# version 01
# baseline (speedup 1.0000x reference)
import os
import numpy as np
from contextlib import ExitStack

import concourse.bass as bass
import concourse.mybir as mybir
from concourse.bass_utils import run_bass_kernel_spmd

F32 = mybir.dt.float32
BF16 = mybir.dt.bfloat16
AF = mybir.ActivationFunctionType
ALU = mybir.AluOpType
AX = mybir.AxisListType.X

HID = 4096
INTER = 8192
NC = 8
SEG = 2048
HALO = 512
SPAN = HALO + 3 + SEG
NCHUNK = (HALO + SEG) // 128
CI0 = HALO // 128
JC_ALL = 145
NBD = 81 * 128
EPS = 1e-5

DEBUG_TAPS = os.environ.get("MK_TAPS", "0") == "1"
MK_PASSES = os.environ.get("MK_PASSES", "ABCD")
MK_DCUT = int(os.environ.get("MK_DCUT", "9"))
_CACHE = {}


def AP(t, off, dims):
    return bass.AP(t, off, dims)


class Trk:
    def __init__(self, stack, nc, name):
        self.h = stack.enter_context(nc.semaphore(name))
        self.n = 0

    def inc(self, k=1):
        self.n += k
        return self.h



_DATAPATH = {
    "tensor_tensor", "tensor_scalar", "tensor_copy", "tensor_add", "tensor_sub",
    "tensor_mul", "tensor_max", "tensor_scalar_mul", "tensor_scalar_add",
    "tensor_single_scalar", "scalar_tensor_tensor", "tensor_tensor_scan",
    "tensor_reduce", "reduce_max", "reduce_sum", "reciprocal", "memset",
    "memzero", "iota", "activation", "select", "transpose",
}


class _WPE:

    def __init__(self, eng):
        self._e = eng
        self.last = None

    def __getattr__(self, name):
        f = getattr(self._e, name)
        if name in ("matmul", "transpose", "ldweights"):
            def g(*a, **k):
                ins = f(*a, **k)
                if name != "ldweights":
                    self.last = ins
                return ins
            return g
        return f


class _W:

    def __init__(self, eng, trk):
        self._e = eng
        self._t = trk

    def __getattr__(self, name):
        f = getattr(self._e, name)
        if name in _DATAPATH:
            trk = self._t
            eng = self._e

            def g(*a, **k):
                self._since = getattr(self, "_since", 0) + 1
                if self._since >= 4 and trk.n >= 1:
                    eng.wait_ge(trk.h, trk.n)
                    self._since = 0
                ins = f(*a, **k)
                ins.then_inc(trk.inc(), 1)
                return ins
            return g
        return f


def _finish(nc, blk, trks):
    @blk.sync
    def _(s):
        for t in trks:
            s.wait_ge(t.h, t.n)


def build_nc():
    nc = bass.Bass("TRN2", target_bir_lowering=False)

    xT = nc.dram_tensor("xT", [HID, SPAN], F32, kind="ExternalInput").ap()
    w1t = nc.dram_tensor("w1t", [JC_ALL, HID, 128], F32, kind="ExternalInput").ap()
    w2t = nc.dram_tensor("w2t", [32, INTER, 128], F32, kind="ExternalInput").ap()
    convw = nc.dram_tensor("convw", [128, 320], F32, kind="ExternalInput").ap()
    convb = nc.dram_tensor("convb", [128, 80], F32, kind="ExternalInput").ap()
    dtbias = nc.dram_tensor("dtbias", [128], F32, kind="ExternalInput").ap()
    alog = nc.dram_tensor("alog", [128], F32, kind="ExternalInput").ap()
    dexp = nc.dram_tensor("dexp", [128, 64], F32, kind="ExternalInput").ap()
    gam = nc.dram_tensor("gam", [128, 64], F32, kind="ExternalInput").ap()
    rwt = nc.dram_tensor("rwt", [HID, 8], F32, kind="ExternalInput").ap()
    atw = nc.dram_tensor("atw", [HID, 128], F32, kind="ExternalInput").ap()
    bwt = nc.dram_tensor("bwt", [128, HID], F32, kind="ExternalInput").ap()
    hmask = nc.dram_tensor("hmask", [128], F32, kind="ExternalInput").ap()

    outT = nc.dram_tensor("outT", [HID, SEG], F32, kind="ExternalOutput").ap()

    spk = "ExternalOutput" if DEBUG_TAPS else "Internal"
    proj_sp = nc.dram_tensor("proj_sp", [NBD, SPAN], BF16, kind=spk).ap()
    gate_sp = nc.dram_tensor("gate_sp", [INTER, SEG], BF16, kind=spk).ap()
    normed_sp = nc.dram_tensor("normed_sp", [INTER, SEG], BF16, kind=spk).ap()
    z_sp = nc.dram_tensor("z_sp", [HID, SEG], F32, kind=spk).ap()
    stg_d = nc.dram_tensor("stg_d", [2, 128, 128], BF16, kind="Internal").ap()
    stgL_d = nc.dram_tensor("stgL_d", [128], F32, kind="Internal").ap()
    rstd_d = nc.dram_tensor("rstd_d", [8, 128], BF16, kind="Internal").ap()

    st = ExitStack()
    e = st.enter_context

    def tk(name):
        return Trk(st, nc, name)

    sC = tk("sC")
    sGp = tk("sGp")
    sPe = tk("sPe")
    sDv = tk("sDv")
    sAc = tk("sAc")
    sSt = tk("sSt")
    sAx = tk("sAx")
    sAw = [tk("sAw0"), tk("sAw1")]
    sAmm = tk("sAmm")
    sAev = tk("sAev")
    sAst = [tk("sAst0"), tk("sAst1")]
    sPj = [tk("sPj0"), tk("sPj1")]
    sGt = [tk("sGt0"), tk("sGt1")]
    sNw = tk("sNw")
    sCw = [tk("sCw0"), tk("sCw1")]
    sCx = [tk("sCx0"), tk("sCx1")]
    sCst = [tk("sCst0"), tk("sCst1")]
    sZl = tk("sZl")
    sDw = tk("sDw")
    sOut = tk("sOut")

    convw_s = e(nc.sbuf_tensor("convw_s", [128, 320], F32))
    convb_s = e(nc.sbuf_tensor("convb_s", [128, 80], F32))
    dtb_s = e(nc.sbuf_tensor("dtb_s", [128, 1], F32))
    alog_s = e(nc.sbuf_tensor("alog_s", [128, 1], F32))
    negA_s = e(nc.sbuf_tensor("negA_s", [128, 1], F32))
    dexp_s = e(nc.sbuf_tensor("dexp_s", [128, 64], F32))
    gam_s = e(nc.sbuf_tensor("gam_s", [128, 64], F32))
    mask_s = e(nc.sbuf_tensor("mask_s", [128, 1], F32))
    onesb = e(nc.sbuf_tensor("onesb", [1, 128], BF16))
    onesf = e(nc.sbuf_tensor("onesf", [1, 128], F32))
    onescol = e(nc.sbuf_tensor("onescol", [128, 1], BF16))
    identb = e(nc.sbuf_tensor("identb", [128, 128], BF16))
    identf = e(nc.sbuf_tensor("identf", [128, 128], F32))
    trilC = e(nc.sbuf_tensor("trilC", [128, 128], BF16))
    zsel = e(nc.sbuf_tensor("zsel", [128, 16], BF16))
    rep_s = e(nc.sbuf_tensor("rep_s", [8, 128], BF16))
    epsc = e(nc.sbuf_tensor("epsc", [128, 1], F32))

    nc.const_aps.register(F32, EPS, epsc[:, :])

    v = _W(nc.vector, sDv)
    sc = _W(nc.scalar, sAc)
    g = _W(nc.gpsimd, sGp)
    t = _WPE(nc.tensor)

    blk = e(nc.Block())

    @blk.sync
    def _(s):
        s.dma_start(out=convw_s[:, :], in_=convw).then_inc(sC.inc(16), 16)
        s.dma_start(out=convb_s[:, :], in_=convb).then_inc(sC.inc(16), 16)
        s.dma_start(out=dtb_s[:, :], in_=dtbias.rearrange("(p o) -> p o", o=1)).then_inc(sC.inc(16), 16)
        s.dma_start(out=alog_s[:, :], in_=alog.rearrange("(p o) -> p o", o=1)).then_inc(sC.inc(16), 16)
        s.dma_start(out=dexp_s[:, :], in_=dexp).then_inc(sC.inc(16), 16)
        s.dma_start(out=gam_s[:, :], in_=gam).then_inc(sC.inc(16), 16)
        s.dma_start(out=mask_s[:, :], in_=hmask.rearrange("(p o) -> p o", o=1)).then_inc(sC.inc(16), 16)

    with ExitStack() as st0:
        it0 = st0.enter_context(nc.sbuf_tensor("it0", [128, 128], F32))
        it1 = st0.enter_context(nc.sbuf_tensor("it1", [128, 128], F32))
        tmpc_it = st0.enter_context(nc.sbuf_tensor("tmpc_it", [128, 128], F32))

        @blk.gpsimd
        def _(_eng):
            g.memset(epsc[:, :], EPS)
            g.memset(onesb[:, :], 1.0)
            g.memset(onesf[:, :], 1.0)
            g.memset(onescol[:, :], 1.0)
            g.memset(zsel[:, :], 0.0)
            g.wait_ge(sGp.h, sGp.n)
            g.memset(zsel[:, 8:9], 1.0)
            g.iota(it0[:, :], [[1, 128]], channel_multiplier=0,
                   allow_small_or_imprecise_dtypes=True)
            g.iota(it1[:, :], [[0, 128]], channel_multiplier=1,
                   allow_small_or_imprecise_dtypes=True)
            g.wait_ge(sGp.h, sGp.n)

        @blk.vector
        def _(_eng):
            v.wait_ge(sGp.h, sGp.n)
            v.tensor_tensor(identf[:, :], it0[:, :], it1[:, :], ALU.is_equal)
            v.wait_ge(sDv.h, sDv.n)
            v.tensor_copy(identb[:, :], identf[:, :])
            v.tensor_tensor(trilC[:, :], it0[:, :], it1[:, :], ALU.is_ge)
            v.wait_ge(sDv.h, sDv.n)

        @blk.gpsimd
        def _(_eng):
            g.wait_ge(sDv.h, sDv.n)
            g.iota(it1[:, :], [[1, 128]], channel_multiplier=-16,
                   allow_small_or_imprecise_dtypes=True)
            g.wait_ge(sGp.h, sGp.n)

        @blk.vector
        def _(_eng):
            v.wait_ge(sGp.h, sGp.n)
            v.tensor_scalar(it0[:, :], it1[:, :], 0.0, None, ALU.is_ge)
            v.tensor_scalar(tmpc_it[:, :], it1[:, :], 15.0, None, ALU.is_le)
            v.wait_ge(sDv.h, sDv.n)
            v.tensor_mul(it0[:, :], it0[:, :], tmpc_it[:, :])
            v.wait_ge(sDv.h, sDv.n)
            v.tensor_copy(rep_s[:, :], it0[0:8, :])
            v.wait_ge(sDv.h, sDv.n)

        @blk.scalar
        def _(_eng):
            sc.wait_ge(sC.h, sC.n)
            sc.wait_ge(sDv.h, sDv.n)
            sc.activation(negA_s[:, :], alog_s[:, :], AF.Exp)
            sc.wait_ge(sAc.h, sAc.n)

        @blk.vector
        def _(_eng):
            v.wait_ge(sAc.h, sAc.n)
            v.tensor_scalar_mul(negA_s[:, :], negA_s[:, :], -1.0)
            v.wait_ge(sDv.h, sDv.n)

    with ExitStack() as stA:
        eA = stA.enter_context
        xtb = eA(nc.sbuf_tensor("xtb", [128, 32 * SPAN], BF16))
        wbufA = [eA(nc.sbuf_tensor(f"wbufA{i}", [128, 32 * 128], BF16)) for i in range(2)]
        stagA = eA(nc.sbuf_tensor("stagA", [128, 6 * 512], BF16))
        pA = [eA(nc.psum_tensor(f"pA{i}", [128, 512], F32)) for i in range(6)]

        @blk.gpsimd
        def _(_eng):
            g.wait_ge(sDv.h, sDv.n)
            g.wait_ge(sAc.h, sAc.n)
            for i in range(8):
                src = AP(xT.tensor, i * 4 * 128 * SPAN,
                         [[SPAN, 128], [128 * SPAN, 4], [1, SPAN]])
                dst = AP(xtb, i * 4 * SPAN, [[32 * SPAN, 128], [SPAN, 4], [1, SPAN]])
                g.dma_start(out=dst, in_=src).then_inc(sAx.inc(16), 16)

        jc_order = list(range(64, 145)) + list(range(0, 64))

        def ntiles_for(jc):
            if jc >= 64:
                return [(0, 512), (512, 512), (1024, 512), (1536, 512),
                        (2048, 512), (2560, 3)]
            return [(515, 512), (1027, 512), (1539, 512), (2051, 512)]

        for ji, jc in enumerate(jc_order):
            par = ji % 2
            nts = ntiles_for(jc)

            @blk.gpsimd
            def _(_eng, ji=ji, jc=jc, par=par):
                if ji >= 2:
                    g.wait_ge(sAmm.h, ji - 1)
                src = AP(w1t.tensor, jc * HID * 128,
                         [[128, 128], [128 * 128, 32], [1, 128]])
                dst = AP(wbufA[par], 0, [[32 * 128, 128], [128, 32], [1, 128]])
                g.dma_start(out=dst, in_=src).then_inc(sAw[par].inc(16), 16)

            @blk.tensor
            def _(_eng, ji=ji, jc=jc, par=par, nts=nts):
                if ji == 0:
                    t.wait_ge(sAx.h, sAx.n)
                t.wait_ge(sAw[par].h, sAw[par].n)
                if ji >= 1:
                    t.wait_ge(sDv.h, sDv.n)
                for k in range(32):
                    t.ldweights(wbufA[par][:, k * 128:(k + 1) * 128])
                    for b, (t0, nn) in enumerate(nts):
                        t.matmul(pA[b][:, 0:nn],
                                 wbufA[par][:, k * 128:(k + 1) * 128],
                                 xtb[:, k * SPAN + t0:k * SPAN + t0 + nn],
                                 start=(k == 0), stop=(k == 31))
                t.last.then_inc(sAmm.inc(), 1)

            @blk.vector
            def _(_eng, ji=ji, nts=nts, par=par):
                v.wait_ge(sAmm.h, ji + 1)
                if ji >= 1:
                    v.wait_ge(sAst[(ji - 1) % 2].h, sAst[(ji - 1) % 2].n)
                for b, (t0, nn) in enumerate(nts):
                    v.tensor_copy(stagA[:, b * 512:b * 512 + nn], pA[b][:, 0:nn])
                v.nop().then_inc(sAev.inc(), 1)

            @blk.sync
            def _(s, ji=ji, jc=jc, nts=nts, par=par):
                s.wait_ge(sDv.h, sDv.n)
                for b, (t0, nn) in enumerate(nts):
                    if jc >= 64:
                        dst = AP(proj_sp.tensor, (jc - 64) * 128 * SPAN + t0,
                                 [[SPAN, 128], [1, nn]])
                    else:
                        dst = AP(gate_sp.tensor, jc * 128 * SEG + (t0 - 515),
                                 [[SEG, 128], [1, nn]])
                    s.dma_start(out=dst,
                                in_=AP(stagA, b * 512, [[6 * 512, 128], [1, nn]])
                                ).then_inc(sAst[par].inc(16), 16)

    if "B" not in MK_PASSES:
        _finish(nc, blk, [sAst[0], sAst[1], sSt, sOut])
        st.close()
        return nc
    with ExitStack() as stB:
        eB = stB.enter_context
        projb = eB(nc.sbuf_tensor("projb", [128, 81 * 131], BF16))
        gateb = eB(nc.sbuf_tensor("gateb", [128, INTER], BF16))
        cacc = eB(nc.sbuf_tensor("cacc", [128, 16 * 128], F32))
        tcv = eB(nc.sbuf_tensor("tcv", [128, 8 * 128], F32))
        hsF = eB(nc.sbuf_tensor("hsF", [128, INTER], BF16))
        BCF = eB(nc.sbuf_tensor("BCF", [128, 2048], BF16))
        hs_sp = eB(nc.sbuf_tensor("hs_sp", [128, INTER], BF16))
        B_sp = eB(nc.sbuf_tensor("B_sp", [128, 1024], BF16))
        Xd = eB(nc.sbuf_tensor("Xd", [128, INTER], BF16))
        stateF = eB(nc.sbuf_tensor("stateF", [128, INTER], F32))
        state_b = eB(nc.sbuf_tensor("state_b", [128, INTER], BF16))
        dt_f = eB(nc.sbuf_tensor("dt_f", [128, 128], F32))
        dtT = eB(nc.sbuf_tensor("dtT", [128, 128], F32))
        acs = eB(nc.sbuf_tensor("acs", [128, 128], F32))
        acsT = eB(nc.sbuf_tensor("acsT", [128, 128], F32))
        hi_t = eB(nc.sbuf_tensor("hi_t", [128, 128], BF16))
        lo_t = eB(nc.sbuf_tensor("lo_t", [128, 128], BF16))
        shalf = [eB(nc.sbuf_tensor(f"shalf{i}", [1, 4096], BF16)) for i in range(2)]
        stgLrow = eB(nc.sbuf_tensor("stgLrow", [1, 128], F32))
        ddT = eB(nc.sbuf_tensor("ddT", [128, 128], F32))
        dcrep = eB(nc.sbuf_tensor("dcrep", [128, 128], F32))
        tmpa = eB(nc.sbuf_tensor("tmpa", [128, 128], F32))
        tmpb = eB(nc.sbuf_tensor("tmpb", [128, 128], F32))
        cbt = eB(nc.sbuf_tensor("cbt", [128, 8 * 128], F32))
        t1t = eB(nc.sbuf_tensor("t1t", [128, 8 * 128], F32))
        ebuf = eB(nc.sbuf_tensor("ebuf", [128, 8 * 128], BF16))
        s1buf = eB(nc.sbuf_tensor("s1buf", [128, 8 * 128], BF16))
        mful = [eB(nc.sbuf_tensor(f"mful{i}", [128, 8 * 128], BF16)) for i in range(2)]
        csdo = [eB(nc.sbuf_tensor(f"csdo{i}", [128, 8 * 128], BF16)) for i in range(2)]
        tpair = eB(nc.sbuf_tensor("tpair", [128, 4 * 128], F32))
        gsil = eB(nc.sbuf_tensor("gsil", [128, 4 * 128], BF16))
        rstdg = eB(nc.sbuf_tensor("rstdg", [8, 128], F32))
        rstdg2 = eB(nc.sbuf_tensor("rstdg2", [8, 128], F32))
        rstdgb = eB(nc.sbuf_tensor("rstdgb", [8, 128], BF16))
        rrow = eB(nc.sbuf_tensor("rrow", [1, 1024], BF16))
        nstag = eB(nc.sbuf_tensor("nstag", [128, 8 * 128], BF16))

        pT = [eB(nc.psum_tensor(f"pT{i}", [128, 1024], BF16)) for i in range(2)]
        pF = [eB(nc.psum_tensor(f"pF{i}", [128, 512], F32)) for i in range(2)]
        pLa = eB(nc.psum_tensor("pLa", [128, 512], F32))
        pLb = eB(nc.psum_tensor("pLb", [128, 512], F32))
        pY = [eB(nc.psum_tensor(f"pY{i}", [128, 512], F32)) for i in range(2)]

        @blk.vector
        def _(_eng):
            v.memset(stateF[:, :], 0.0)
            v.memset(state_b[:, :], 0.0)
            v.wait_ge(sDv.h, sDv.n)

        conv_order = list(range(64, 80)) + list(range(0, 64))

        for ci in range(NCHUNK):
            real = ci >= CI0
            par = ci % 2

            @blk.sync
            def _(s, ci=ci, real=real, par=par):
                if ci == 0:
                    s.wait_ge(sAst[0].h, sAst[0].n)
                    s.wait_ge(sAst[1].h, sAst[1].n)
                if ci >= 1:
                    s.wait_ge(sGp.h, sGp.n)
                    s.wait_ge(sDv.h, sDv.n)
                src = AP(proj_sp.tensor, ci * 128,
                         [[SPAN, 128], [128 * SPAN, 81], [1, 131]])
                dst = AP(projb, 0, [[81 * 131, 128], [131, 81], [1, 131]])
                s.dma_start(out=dst, in_=src).then_inc(sPj[par].inc(16), 16)
                if real:
                    srcg = AP(gate_sp.tensor, (ci - CI0) * 128,
                              [[SEG, 128], [128 * SEG, 64], [1, 128]])
                    dstg = AP(gateb, 0, [[INTER, 128], [128, 64], [1, 128]])
                    s.dma_start(out=dstg, in_=srcg).then_inc(sGt[par].inc(16), 16)

            @blk.vector
            def _(_eng, ci=ci, par=par):
                v.wait_ge(sPj[par].h, sPj[par].n)
                if ci == CI0:
                    m3 = AP(projb, 0, [[81 * 131, 128], [131, 81], [1, 3]])
                    v.tensor_scalar(m3, m3, mask_s[:, :], None, ALU.mult)
                dts = 80 * 131
                v.tensor_scalar(tmpa[:, :], projb[:, dts + 3:dts + 131],
                                dtb_s[:, :], None, ALU.add)
                v.wait_ge(sDv.h, sDv.n)

            @blk.scalar
            def _(_eng, ci=ci):
                sc.wait_ge(sDv.h, sDv.n)
                sc.activation(tmpb[:, :], tmpa[:, :], AF.Abs)
                sc.wait_ge(sAc.h, sAc.n)
                sc.activation(tmpb[:, :], tmpb[:, :], AF.Exp, scale=-1.0)
                sc.wait_ge(sAc.h, sAc.n)
                sc.activation(tmpb[:, :], tmpb[:, :], AF.Ln, bias=1.0)
                sc.wait_ge(sAc.h, sAc.n)

            @blk.vector
            def _(_eng, ci=ci):
                v.wait_ge(sAc.h, sAc.n)
                v.tensor_scalar(dt_f[:, :], tmpa[:, :], 0.0, None, ALU.max)
                v.wait_ge(sDv.h, sDv.n)
                v.tensor_add(dt_f[:, :], dt_f[:, :], tmpb[:, :])
                v.wait_ge(sDv.h, sDv.n)
                v.tensor_scalar(tmpa[:, :], dt_f[:, :], negA_s[:, :], None, ALU.mult)
                v.wait_ge(sDv.h, sDv.n)
                v.tensor_tensor_scan(acs[:, :], tmpa[:, :], tmpa[:, :], 0.0,
                                     ALU.add, ALU.bypass)
                v.wait_ge(sDv.h, sDv.n)
                v.tensor_copy(hi_t[:, :], acs[:, :])
                v.wait_ge(sDv.h, sDv.n)
                v.tensor_tensor(lo_t[:, :], acs[:, :], hi_t[:, :], ALU.subtract)
                v.wait_ge(sDv.h, sDv.n)

            @blk.tensor
            def _(_eng, ci=ci):
                t.wait_ge(sDv.h, sDv.n)
                t.transpose(pF[0][:, 0:128], acs[:, :], identf[:, :])
                t.transpose(pF[0][:, 128:256], dt_f[:, :], identf[:, :])
                t.last.then_inc(sPe.inc(), 1)

            @blk.vector
            def _(_eng, ci=ci):
                v.wait_ge(sPe.h, sPe.n)
                v.tensor_copy(acsT[:, :], pF[0][:, 0:128])
                v.tensor_copy(dtT[:, :], pF[0][:, 128:256])
                v.wait_ge(sDv.h, sDv.n)

            @blk.sync
            def _(s, ci=ci):
                s.wait_ge(sDv.h, sDv.n)
                s.wait_ge(sSt.h, sSt.n)
                s.dma_start(out=stg_d[0], in_=hi_t[:, :]).then_inc(sSt.inc(16), 16)
                s.dma_start(out=stg_d[1], in_=lo_t[:, :]).then_inc(sSt.inc(16), 16)
                s.dma_start(out=stgL_d, in_=acs[:, 127:128]).then_inc(sSt.inc(16), 16)
                s.wait_ge(sSt.h, sSt.n)
                s.dma_start(out=shalf[0][0:1, :],
                            in_=AP(stg_d.tensor, 0, [[16384, 1], [1, 4096]]),
                            single_packet=True).then_inc(sSt.inc(16), 16)
                s.dma_start(out=shalf[1][0:1, :],
                            in_=AP(stg_d.tensor, 16384, [[16384, 1], [1, 4096]]),
                            single_packet=True).then_inc(sSt.inc(16), 16)
                s.dma_start(out=stgLrow[0:1, :],
                            in_=AP(stgL_d.tensor, 0, [[128, 1], [1, 128]]),
                            single_packet=True).then_inc(sSt.inc(16), 16)

            @blk.tensor
            def _(_eng, ci=ci):
                t.wait_ge(sSt.h, sSt.n)
                t.matmul(pF[1][:, 0:128], onesf[0:1, :], stgLrow[0:1, :],
                         start=True, stop=True)
                t.last.then_inc(sPe.inc(), 1)

            @blk.vector
            def _(_eng, ci=ci):
                v.wait_ge(sPe.h, sPe.n)
                v.tensor_tensor(tmpb[:, :], pF[1][:, 0:128], acsT[:, :], ALU.subtract)
                v.wait_ge(sDv.h, sDv.n)

            @blk.scalar
            def _(_eng, ci=ci):
                sc.wait_ge(sDv.h, sDv.n)
                sc.activation(dcrep[:, :], pF[1][:, 0:128], AF.Exp)
                sc.activation(tmpa[:, :], tmpb[:, :], AF.Exp)
                sc.wait_ge(sAc.h, sAc.n)

            @blk.vector
            def _(_eng, ci=ci):
                v.wait_ge(sAc.h, sAc.n)
                v.tensor_mul(ddT[:, :], tmpa[:, :], dtT[:, :])
                v.wait_ge(sDv.h, sDv.n)

            silu_done = []
            for wv in range(10):
                tiles = conv_order[wv * 8:(wv + 1) * 8]

                @blk.gpsimd
                def _(_eng, ci=ci, par=par, wv=wv, tiles=tiles, silu_done=silu_done):
                    if wv == 0:
                        g.wait_ge(sPj[par].h, sPj[par].n)
                        g.wait_ge(sDv.h, sDv.n)
                    if wv >= 2:
                        g.wait_ge(sAc.h, silu_done[wv - 2])
                    def wb(jt, k):
                        return AP(convw_s, 4 * jt + k, [[320, 128], [0, 128]])

                    def asl(i):
                        sl = (wv % 2) * 8 + i
                        return cacc[:, sl * 128:(sl + 1) * 128]
                    for i, jt in enumerate(tiles):
                        g.tensor_tensor(asl(i), projb[:, jt * 131:jt * 131 + 128],
                                        wb(jt, 0), ALU.mult)
                    for k in range(1, 4):
                        for i, jt in enumerate(tiles):
                            g.tensor_tensor(tcv[:, i * 128:(i + 1) * 128],
                                            projb[:, jt * 131 + k:jt * 131 + k + 128],
                                            wb(jt, k), ALU.mult)
                        for i, jt in enumerate(tiles):
                            g.tensor_tensor(asl(i), asl(i),
                                            tcv[:, i * 128:(i + 1) * 128], ALU.add)
                    g.wait_ge(sGp.h, sGp.n)

                @blk.scalar
                def _(_eng, ci=ci, wv=wv, tiles=tiles):
                    sc.wait_ge(sGp.h, sGp.n)
                    if wv == 0 and ci >= 1:
                        sc.wait_ge(sPe.h, sPe.n)
                    for i, jt in enumerate(tiles):
                        sl = (wv % 2) * 8 + i
                        a = cacc[:, sl * 128:(sl + 1) * 128]
                        if jt < 64:
                            dst = hsF[:, jt * 128:(jt + 1) * 128]
                        else:
                            dst = BCF[:, (jt - 64) * 128:(jt - 63) * 128]
                        sc.activation(dst, a, AF.Silu, bias=convb_s[:, jt:jt + 1])
                    sc.wait_ge(sAc.h, sAc.n)
                silu_done.append(sAc.n)

            for wv in range(9):
                @blk.tensor
                def _(_eng, ci=ci, wv=wv, silu_done=silu_done):
                    if wv == 0:
                        t.wait_ge(sAc.h, silu_done[0])
                        t.wait_ge(sDv.h, sDv.n)
                    else:
                        t.wait_ge(sAc.h, silu_done[min(wv + 1, 9)])
                    if wv >= 2:
                        t.wait_ge(sDv.h, sDv.n)
                    pb = pT[wv % 2]
                    if wv == 0:
                        for i in range(8):
                            t.transpose(pb[:, i * 128:(i + 1) * 128],
                                        BCF[:, i * 128:(i + 1) * 128], identb[:, :])
                    else:
                        for i in range(8):
                            jt = (wv - 1) * 8 + i
                            t.transpose(pb[:, i * 128:(i + 1) * 128],
                                        hsF[:, jt * 128:(jt + 1) * 128], identb[:, :])
                    t.last.then_inc(sPe.inc(), 1)

                @blk.vector
                def _(_eng, ci=ci, wv=wv):
                    v.wait_ge(sPe.h, sPe.n)
                    pb = pT[wv % 2]
                    if wv == 0:
                        v.tensor_copy(B_sp[:, :], pb[:, :])
                    else:
                        v.tensor_copy(hs_sp[:, (wv - 1) * 1024:wv * 1024], pb[:, :])
                    v.wait_ge(sDv.h, sDv.n)

            @blk.vector
            def _(_eng, ci=ci):
                v.wait_ge(sDv.h, sDv.n)
                dd_ap = AP(ddT, 0, [[128, 128], [1, 128], [0, 64]])
                v.tensor_tensor(Xd[:, :], hs_sp[:, :], dd_ap, ALU.mult)
                v.wait_ge(sDv.h, sDv.n)

            if real:
                if ci == CI0:
                    @blk.vector
                    def _(_eng):
                        v.tensor_scalar(stateF[:, :], stateF[:, :], mask_s[:, :],
                                        None, ALU.mult)
                        v.tensor_scalar(state_b[:, :], state_b[:, :], mask_s[:, :],
                                        None, ALU.mult)
                        v.wait_ge(sDv.h, sDv.n)

                @blk.tensor
                def _(_eng, ci=ci):
                    t.wait_ge(sDv.h, sDv.n)
                    for g8 in range(8):
                        pb = pF[g8 // 4]
                        off = (g8 % 4) * 128
                        t.ldweights(BCF[:, g8 * 128:(g8 + 1) * 128])
                        t.matmul(pb[:, off:off + 128],
                                 BCF[:, g8 * 128:(g8 + 1) * 128],
                                 BCF[:, (8 + g8) * 128:(9 + g8) * 128],
                                 start=(g8 % 4 == 0), stop=(g8 % 4 == 3))
                    t.last.then_inc(sPe.inc(), 1)

                @blk.vector
                def _(_eng, ci=ci):
                    v.wait_ge(sPe.h, sPe.n)
                    for g8 in range(8):
                        pb = pF[g8 // 4]
                        off = (g8 % 4) * 128
                        v.tensor_tensor(cbt[:, g8 * 128:(g8 + 1) * 128],
                                        pb[:, off:off + 128], trilC[:, :], ALU.mult)
                    v.wait_ge(sDv.h, sDv.n)

                for w in range(17):
                    if w in (4, 8, 12):
                        @blk.sync
                        def _(s, ci=ci, w=w):
                            s.wait_ge(sPe.h, sPe.n)
                            q = (w // 4) * 4096
                            s.dma_start(out=shalf[0][0:1, :],
                                        in_=AP(stg_d.tensor, q, [[16384, 1], [1, 4096]]),
                                        single_packet=True).then_inc(sSt.inc(16), 16)
                            s.dma_start(out=shalf[1][0:1, :],
                                        in_=AP(stg_d.tensor, 16384 + q, [[16384, 1], [1, 4096]]),
                                        single_packet=True).then_inc(sSt.inc(16), 16)

                    @blk.tensor
                    def _(_eng, ci=ci, w=w):
                        if w < 16:
                            t.wait_ge(sDv.h, sDv.n)
                            t.wait_ge(sAc.h, sAc.n)
                            if w in (4, 8, 12):
                                t.wait_ge(sSt.h, sSt.n)
                            t.ldweights(onesb[0:1, :])
                            for i in range(8):
                                h = 8 * w + i
                                pb = pLa if i < 4 else pLb
                                off = (i % 4) * 128
                                hh = h % 32
                                t.matmul(pb[:, off:off + 128], onesb[0:1, :],
                                         shalf[0][0:1, hh * 128:(hh + 1) * 128],
                                         start=(i % 4 == 0), stop=False)
                                t.matmul(pb[:, off:off + 128], onesb[0:1, :],
                                         shalf[1][0:1, hh * 128:(hh + 1) * 128],
                                         start=False, stop=(i % 4 == 3))
                        if w >= 1:
                            wp = (w - 1) % 2
                            t.wait_ge(sDv.h, sDv.n)
                            pyb = pY[wp]
                            for p4 in range(4):
                                pg = 4 * (w - 1) + p4
                                for sub in range(2):
                                    h = 2 * pg + sub
                                    i = h - 8 * (w - 1)
                                    t.ldweights(hs_sp[:, h * 64:(h + 1) * 64])
                                    t.matmul(pyb[64 * sub:64 * (sub + 1), p4 * 128:(p4 + 1) * 128],
                                             hs_sp[:, h * 64:(h + 1) * 64],
                                             mful[wp][:, i * 128:(i + 1) * 128],
                                             start=(p4 == 0), stop=False,
                                             skip_group_check=True)
                                    t.ldweights(state_b[:, h * 64:(h + 1) * 64])
                                    t.matmul(pyb[64 * sub:64 * (sub + 1), p4 * 128:(p4 + 1) * 128],
                                             state_b[:, h * 64:(h + 1) * 64],
                                             csdo[wp][:, i * 128:(i + 1) * 128],
                                             start=False,
                                             stop=(p4 == 3),
                                             skip_group_check=True)
                        t.last.then_inc(sPe.inc(), 1)

                    if w < 16:
                        @blk.vector
                        def _(_eng, ci=ci, w=w):
                            v.wait_ge(sPe.h, sPe.n)
                            for i in range(8):
                                h = 8 * w + i
                                pb = pLa if i < 4 else pLb
                                off = (i % 4) * 128
                                v.tensor_scalar(t1t[:, i * 128:(i + 1) * 128],
                                                pb[:, off:off + 128],
                                                acsT[:, h:h + 1], 87.0,
                                                ALU.subtract, ALU.min)
                            v.wait_ge(sDv.h, sDv.n)

                        @blk.scalar
                        def _(_eng, ci=ci, w=w):
                            sc.wait_ge(sDv.h, sDv.n)
                            for i in range(8):
                                pb = pLa if i < 4 else pLb
                                off = (i % 4) * 128
                                sc.activation(ebuf[:, i * 128:(i + 1) * 128],
                                              t1t[:, i * 128:(i + 1) * 128], AF.Exp)
                                sc.activation(s1buf[:, i * 128:(i + 1) * 128],
                                              pb[:, off:off + 128], AF.Exp)
                            sc.wait_ge(sAc.h, sAc.n)

                        @blk.vector
                        def _(_eng, ci=ci, w=w):
                            v.wait_ge(sAc.h, sAc.n)
                            wp = w % 2
                            for i in range(8):
                                h = 8 * w + i
                                v.scalar_tensor_tensor(
                                    mful[wp][:, i * 128:(i + 1) * 128],
                                    cbt[:, i * 128:(i + 1) * 128],
                                    dtT[:, h:h + 1],
                                    ebuf[:, i * 128:(i + 1) * 128],
                                    ALU.mult, ALU.mult)
                                v.tensor_tensor(
                                    csdo[wp][:, i * 128:(i + 1) * 128],
                                    BCF[:, (8 + i) * 128:(9 + i) * 128],
                                    s1buf[:, i * 128:(i + 1) * 128], ALU.mult)
                            v.wait_ge(sDv.h, sDv.n)

                    if w >= 1:
                        @blk.scalar
                        def _(_eng, ci=ci, w=w, par=par):
                            if w == 1:
                                sc.wait_ge(sGt[par].h, sGt[par].n)
                            if w >= 3:
                                sc.wait_ge(sDv.h, sDv.n)
                            for p4 in range(4):
                                pg = 4 * (w - 1) + p4
                                sc.activation(gsil[:, p4 * 128:(p4 + 1) * 128],
                                              gateb[:, pg * 128:(pg + 1) * 128],
                                              AF.Silu)
                            sc.wait_ge(sAc.h, sAc.n)

                        @blk.vector
                        def _(_eng, ci=ci, w=w):
                            v.wait_ge(sPe.h, sPe.n)
                            v.wait_ge(sAc.h, sAc.n)
                            wp = (w - 1) % 2
                            pyb = pY[wp]
                            for p4 in range(4):
                                pg = 4 * (w - 1) + p4
                                v.scalar_tensor_tensor(
                                    tpair[:, p4 * 128:(p4 + 1) * 128],
                                    hsF[:, pg * 128:(pg + 1) * 128],
                                    dexp_s[:, pg:pg + 1],
                                    pyb[:, p4 * 128:(p4 + 1) * 128],
                                    ALU.mult, ALU.add)
                            v.wait_ge(sDv.h, sDv.n)
                            for p4 in range(4):
                                pg = 4 * (w - 1) + p4
                                v.tensor_tensor(gateb[:, pg * 128:(pg + 1) * 128],
                                                tpair[:, p4 * 128:(p4 + 1) * 128],
                                                gsil[:, p4 * 128:(p4 + 1) * 128],
                                                ALU.mult)
                            v.wait_ge(sDv.h, sDv.n)

            @blk.vector
            def _(_eng, ci=ci):
                dc_ap = AP(dcrep, 0, [[128, 128], [1, 128], [0, 64]])
                v.tensor_tensor(stateF[:, :], stateF[:, :], dc_ap, ALU.mult)
                v.wait_ge(sDv.h, sDv.n)

            for g8 in range(8):
                @blk.tensor
                def _(_eng, ci=ci, g8=g8):
                    if g8 == 0:
                        t.wait_ge(sDv.h, sDv.n)
                    elif g8 >= 2:
                        t.wait_ge(sDv.h, sDv.n)
                    pb = pF[g8 % 2]
                    t.ldweights(B_sp[:, g8 * 128:(g8 + 1) * 128])
                    for i in range(8):
                        h = g8 + 8 * i
                        t.matmul(pb[:, i * 64:(i + 1) * 64],
                                 B_sp[:, g8 * 128:(g8 + 1) * 128],
                                 Xd[:, h * 64:(h + 1) * 64],
                                 start=(i == 0), stop=(i == 7))
                    t.last.then_inc(sPe.inc(), 1)

                @blk.vector
                def _(_eng, ci=ci, g8=g8):
                    v.wait_ge(sPe.h, sPe.n)
                    pb = pF[g8 % 2]
                    sl = AP(stateF, g8 * 64, [[INTER, 128], [512, 8], [1, 64]])
                    v.tensor_add(sl, sl, pb[:, :])
                    v.wait_ge(sDv.h, sDv.n)

            @blk.vector
            def _(_eng, ci=ci):
                v.wait_ge(sDv.h, sDv.n)
                v.tensor_copy(state_b[:, :], stateF[:, :])
                v.wait_ge(sDv.h, sDv.n)

            if real:
                for wv in range(8):
                    @blk.gpsimd
                    def _(_eng, ci=ci, wv=wv):
                        g.wait_ge(sDv.h, sDv.n)
                        if wv >= 1:
                            g.wait_ge(sPe.h, sPe.n)
                        for i in range(8):
                            j = wv * 8 + i
                            g.tensor_mul(nstag[:, i * 128:(i + 1) * 128],
                                         gateb[:, j * 128:(j + 1) * 128],
                                         gateb[:, j * 128:(j + 1) * 128])
                        g.wait_ge(sGp.h, sGp.n)

                    @blk.tensor
                    def _(_eng, ci=ci, wv=wv):
                        t.wait_ge(sGp.h, sGp.n)
                        t.ldweights(zsel[:, (8 - wv):(16 - wv)])
                        for i in range(8):
                            t.matmul(pF[0][0:8, 0:128],
                                     zsel[:, (8 - wv):(16 - wv)],
                                     nstag[:, i * 128:(i + 1) * 128],
                                     start=(wv == 0 and i == 0),
                                     stop=(wv == 7 and i == 7))
                        t.last.then_inc(sPe.inc(), 1)

                @blk.vector
                def _(_eng, ci=ci):
                    v.wait_ge(sPe.h, sPe.n)
                    v.tensor_scalar_mul(rstdg[:, :], pF[0][0:8, 0:128], 1.0 / 1024.0)
                    v.wait_ge(sDv.h, sDv.n)

                @blk.scalar
                def _(_eng, ci=ci):
                    sc.wait_ge(sDv.h, sDv.n)
                    sc.activation(rstdg2[:, :], rstdg[:, :], AF.Ln, bias=EPS)
                    sc.wait_ge(sAc.h, sAc.n)

                @blk.vector
                def _(_eng, ci=ci):
                    v.wait_ge(sAc.h, sAc.n)
                    v.tensor_scalar_mul(rstdg[:, :], rstdg2[:, :], -0.5)
                    v.wait_ge(sDv.h, sDv.n)

                @blk.scalar
                def _(_eng, ci=ci):
                    sc.wait_ge(sDv.h, sDv.n)
                    sc.activation(rstdgb[:, :], rstdg[:, :], AF.Exp)
                    sc.wait_ge(sAc.h, sAc.n)

                @blk.sync
                def _(s, ci=ci):
                    s.wait_ge(sAc.h, sAc.n)
                    s.dma_start(out=rstd_d, in_=rstdgb[:, :]).then_inc(sSt.inc(16), 16)
                    s.wait_ge(sSt.h, sSt.n)
                    s.dma_start(out=rrow[0:1, :],
                                in_=AP(rstd_d.tensor, 0, [[1024, 1], [1, 1024]]),
                                single_packet=True).then_inc(sSt.inc(16), 16)

                @blk.tensor
                def _(_eng, ci=ci):
                    t.wait_ge(sSt.h, sSt.n)
                    t.ldweights(onesb[0:1, :])
                    for g8 in range(8):
                        pb = pF[1] if g8 < 4 else pF[0]
                        off = (g8 % 4) * 128
                        t.matmul(pb[:, off:off + 128], onesb[0:1, :],
                                 rrow[0:1, g8 * 128:(g8 + 1) * 128],
                                 start=(g8 % 4 == 0), stop=(g8 % 4 == 3))
                    t.last.then_inc(sPe.inc(), 1)

                for wv in range(8):
                    @blk.vector
                    def _(_eng, ci=ci, wv=wv):
                        if wv == 0:
                            v.wait_ge(sPe.h, sPe.n)
                        else:
                            v.wait_ge(sSt.h, sSt.n)
                        for i in range(8):
                            j = wv * 8 + i
                            g8 = j // 8
                            pb = pF[1] if g8 < 4 else pF[0]
                            off = (g8 % 4) * 128
                            v.scalar_tensor_tensor(
                                nstag[:, i * 128:(i + 1) * 128],
                                gateb[:, j * 128:(j + 1) * 128],
                                gam_s[:, j:j + 1],
                                pb[:, off:off + 128],
                                ALU.mult, ALU.mult)
                        v.wait_ge(sDv.h, sDv.n)

                    @blk.sync
                    def _(s, ci=ci, wv=wv):
                        s.wait_ge(sDv.h, sDv.n)
                        dst = AP(normed_sp.tensor, wv * 1024 * SEG + (ci - CI0) * 128,
                                 [[SEG, 128], [128 * SEG, 8], [1, 128]])
                        src = AP(nstag, 0, [[1024, 128], [128, 8], [1, 128]])
                        s.dma_start(out=dst, in_=src).then_inc(sSt.inc(16), 16)

    if "C" not in MK_PASSES:
        _finish(nc, blk, [sAst[0], sAst[1], sSt, sOut])
        st.close()
        return nc
    stats1 = e(nc.sbuf_tensor("stats1", [1, 2 * SEG], F32))

    with ExitStack() as stC:
        eC = stC.enter_context
        nmh = eC(nc.sbuf_tensor("nmh", [128, 64 * 1024], BF16))
        w2b = [eC(nc.sbuf_tensor(f"w2b{i}", [128, 64 * 128], BF16)) for i in range(2)]
        xtl = [eC(nc.sbuf_tensor(f"xtl{i}", [128, 512], F32)) for i in range(2)]
        zst = [eC(nc.sbuf_tensor(f"zst{i}", [128, 512], F32)) for i in range(2)]
        zbb = eC(nc.sbuf_tensor("zbb", [128, 512], BF16))
        sqc = eC(nc.sbuf_tensor("sqc", [128, 512], BF16))
        pC = [eC(nc.psum_tensor(f"pC{i}", [128, 512], F32)) for i in range(2)]
        pS = [eC(nc.psum_tensor(f"pS{i}", [1, 512], F32)) for i in range(2)]

        @blk.vector
        def _(_eng):
            v.memset(stats1[:, :], 0.0)
            v.wait_ge(sDv.h, sDv.n)

        for th in range(2):
            @blk.sync
            def _(s, th=th):
                if th == 0:
                    s.wait_ge(sSt.h, sSt.n)
                if th == 1:
                    s.wait_ge(sPe.h, sPe.n)
                for i in range(8):
                    src = AP(normed_sp.tensor, i * 1024 * SEG + th * 1024,
                             [[SEG, 128], [128 * SEG, 8], [1, 1024]])
                    dst = AP(nmh, i * 8 * 1024, [[64 * 1024, 128], [1024, 8], [1, 1024]])
                    s.dma_start(out=dst, in_=src).then_inc(sNw.inc(16), 16)

            for oc in range(32):
                po = oc % 2

                @blk.gpsimd
                def _(_eng, th=th, oc=oc, po=po):
                    if th + oc > 0:
                        g.wait_ge(sPe.h, sPe.n)
                    src = AP(w2t.tensor, oc * INTER * 128,
                             [[128, 128], [128 * 128, 64], [1, 128]])
                    dst = AP(w2b[po], 0, [[64 * 128, 128], [128, 64], [1, 128]])
                    g.dma_start(out=dst, in_=src).then_inc(sCw[po].inc(16), 16)

                @blk.sync
                def _(s, th=th, oc=oc, po=po):
                    if th + oc > 0:
                        s.wait_ge(sDv.h, sDv.n)
                    for t2 in range(2):
                        col = 515 + th * 1024 + t2 * 512
                        s.dma_start(out=xtl[t2][:, :],
                                    in_=AP(xT.tensor, oc * 128 * SPAN + col,
                                           [[SPAN, 128], [1, 512]])
                                    ).then_inc(sCx[t2].inc(16), 16)

                @blk.tensor
                def _(_eng, th=th, oc=oc, po=po):
                    t.wait_ge(sCw[po].h, sCw[po].n)
                    if oc == 0:
                        t.wait_ge(sNw.h, sNw.n)
                    if th + oc >= 1:
                        t.wait_ge(sDv.h, sDv.n)
                    for k in range(64):
                        t.ldweights(w2b[po][:, k * 128:(k + 1) * 128])
                        for t2 in range(2):
                            t.matmul(pC[t2][:, :],
                                     w2b[po][:, k * 128:(k + 1) * 128],
                                     nmh[:, k * 1024 + t2 * 512:k * 1024 + (t2 + 1) * 512],
                                     start=(k == 0), stop=(k == 63))
                    t.last.then_inc(sPe.inc(), 1)

                for t2 in range(2):
                    @blk.vector
                    def _(_eng, th=th, oc=oc, t2=t2, po=po):
                        v.wait_ge(sPe.h, sPe.n)
                        v.wait_ge(sCx[t2].h, sCx[t2].n)
                        if th + oc >= 1:
                            v.wait_ge(sCst[t2].h, sCst[t2].n)
                        v.tensor_add(zst[t2][:, :], xtl[t2][:, :], pC[t2][:, :])
                        v.wait_ge(sDv.h, sDv.n)
                        v.tensor_copy(zbb[:, :], zst[t2][:, :])
                        v.wait_ge(sDv.h, sDv.n)

                    @blk.scalar
                    def _(_eng, th=th, oc=oc, t2=t2):
                        sc.wait_ge(sDv.h, sDv.n)
                        sc.activation(sqc[:, :], zst[t2][:, :], AF.Square)
                        sc.wait_ge(sAc.h, sAc.n)

                    @blk.tensor
                    def _(_eng, th=th, oc=oc, t2=t2):
                        t.wait_ge(sAc.h, sAc.n)
                        t.ldweights(onescol[:, 0:1])
                        t.matmul(pS[0][:, :], onescol[:, 0:1], zbb[:, :],
                                 start=True, stop=True)
                        t.matmul(pS[1][:, :], onescol[:, 0:1], sqc[:, :],
                                 start=True, stop=True)
                        t.last.then_inc(sPe.inc(), 1)

                    @blk.vector
                    def _(_eng, th=th, oc=oc, t2=t2):
                        v.wait_ge(sPe.h, sPe.n)
                        c0 = th * 1024 + t2 * 512
                        v.tensor_add(stats1[0:1, c0:c0 + 512],
                                     stats1[0:1, c0:c0 + 512], pS[0][:, :])
                        v.tensor_add(stats1[0:1, SEG + c0:SEG + c0 + 512],
                                     stats1[0:1, SEG + c0:SEG + c0 + 512], pS[1][:, :])
                        v.wait_ge(sDv.h, sDv.n)

                    @blk.sync
                    def _(s, th=th, oc=oc, t2=t2):
                        s.wait_ge(sDv.h, sDv.n)
                        dst = AP(z_sp.tensor, oc * 128 * SEG + th * 1024 + t2 * 512,
                                 [[SEG, 128], [1, 512]])
                        s.dma_start(out=dst, in_=zst[t2][:, :]).then_inc(sCst[t2].inc(16), 16)

    if "D" not in MK_PASSES:
        _finish(nc, blk, [sAst[0], sAst[1], sCst[0], sCst[1], sSt, sOut])
        st.close()
        return nc
    with ExitStack() as stD:
        eD = stD.enter_context
        zb = eD(nc.sbuf_tensor("zb", [128, 32 * 512], F32))
        x1b = eD(nc.sbuf_tensor("x1b", [128, 32 * 512], BF16))
        ffnsb = eD(nc.sbuf_tensor("ffnsb", [128, 32 * 512], BF16))
        atb = eD(nc.sbuf_tensor("atb", [128, 32 * 128], BF16))
        bwtb = eD(nc.sbuf_tensor("bwtb", [128, 32 * 128], BF16))
        rwtb = eD(nc.sbuf_tensor("rwtb", [128, 32 * 8], BF16))
        lrow = eD(nc.sbuf_tensor("lrow", [1, 2 * 512], BF16))
        l8 = eD(nc.sbuf_tensor("l8", [8, 512], BF16))
        lT = eD(nc.sbuf_tensor("lT", [128, 8], BF16))
        mcol = eD(nc.sbuf_tensor("mcol", [128, 1], F32))
        rwq = eD(nc.sbuf_tensor("rwq", [128, 8], BF16))
        rwT = eD(nc.sbuf_tensor("rwT", [8, 512], BF16))
        rw16 = eD(nc.sbuf_tensor("rw16", [128, 512], BF16))
        whb = eD(nc.sbuf_tensor("whb", [128, 512], BF16))
        trow = eD(nc.sbuf_tensor("trow", [1, 512], F32))
        trow2 = eD(nc.sbuf_tensor("trow2", [1, 512], F32))
        stats2 = eD(nc.sbuf_tensor("stats2", [1, 2 * 512], F32))
        tq = [eD(nc.sbuf_tensor(f"tq{i}", [128, 512], F32)) for i in range(2)]
        ost = [eD(nc.sbuf_tensor(f"ost{i}", [128, 512], F32)) for i in range(2)]
        sq2 = eD(nc.sbuf_tensor("sq2", [128, 512], BF16))
        zb2 = eD(nc.sbuf_tensor("zb2", [128, 512], BF16))
        pD = [eD(nc.psum_tensor(f"pD{i}", [128, 512], F32)) for i in range(4)]
        pS2 = [eD(nc.psum_tensor(f"pS2{i}", [1, 512], F32)) for i in range(2)]
        pQ = eD(nc.psum_tensor("pQ", [128, 512], BF16))

        @blk.gpsimd
        def _(_eng):
            g.wait_ge(sPe.h, sPe.n)
            g.wait_ge(sDv.h, sDv.n)
            g.dma_start(out=AP(atb, 0, [[32 * 128, 128], [128, 32], [1, 128]]),
                        in_=AP(atw.tensor, 0, [[128, 128], [128 * 128, 32], [1, 128]])
                        ).then_inc(sDw.inc(16), 16)
            g.dma_start(out=bwtb[:, :], in_=bwt).then_inc(sDw.inc(16), 16)
            g.dma_start(out=AP(rwtb, 0, [[32 * 8, 128], [8, 32], [1, 8]]),
                        in_=AP(rwt.tensor, 0, [[8, 128], [128 * 8, 32], [1, 8]])
                        ).then_inc(sDw.inc(16), 16)

        for b4 in range(4):
            @blk.sync
            def _(s, b4=b4):
                if b4 == 0:
                    s.wait_ge(sCst[0].h, sCst[0].n)
                    s.wait_ge(sCst[1].h, sCst[1].n)
                if b4 >= 1:
                    s.wait_ge(sDv.h, sDv.n)
                for i in range(4):
                    src = AP(z_sp.tensor, i * 8 * 128 * SEG + b4 * 512,
                             [[SEG, 128], [128 * SEG, 8], [1, 512]])
                    dst = AP(zb, i * 8 * 512, [[32 * 512, 128], [512, 8], [1, 512]])
                    s.dma_start(out=dst, in_=src).then_inc(sZl.inc(16), 16)

            @blk.vector
            def _(_eng, b4=b4):
                v.wait_ge(sZl.h, sZl.n)
                c0 = b4 * 512
                v.tensor_scalar_mul(trow[0:1, :], stats1[0:1, c0:c0 + 512], 1.0 / HID)
                v.tensor_scalar_mul(trow2[0:1, :], stats1[0:1, SEG + c0:SEG + c0 + 512],
                                    1.0 / HID)
                v.wait_ge(sDv.h, sDv.n)
                v.tensor_tensor(stats2[0:1, 0:512], trow[0:1, :], trow[0:1, :],
                                ALU.mult)
                v.wait_ge(sDv.h, sDv.n)
                v.tensor_tensor(trow2[0:1, :], trow2[0:1, :], stats2[0:1, 0:512], ALU.subtract)
                v.wait_ge(sDv.h, sDv.n)

            @blk.scalar
            def _(_eng, b4=b4):
                sc.wait_ge(sDv.h, sDv.n)
                sc.activation(trow2[0:1, :], trow2[0:1, :], AF.Ln, bias=EPS)
                sc.wait_ge(sAc.h, sAc.n)

            @blk.vector
            def _(_eng, b4=b4):
                v.wait_ge(sAc.h, sAc.n)
                v.tensor_scalar_mul(trow2[0:1, :], trow2[0:1, :], -0.5)
                v.wait_ge(sDv.h, sDv.n)

            @blk.scalar
            def _(_eng, b4=b4):
                sc.wait_ge(sDv.h, sDv.n)
                sc.activation(trow2[0:1, :], trow2[0:1, :], AF.Exp)
                sc.wait_ge(sAc.h, sAc.n)

            @blk.vector
            def _(_eng, b4=b4):
                v.wait_ge(sAc.h, sAc.n)
                v.tensor_mul(trow[0:1, :], trow[0:1, :], trow2[0:1, :])
                v.tensor_copy(lrow[0:1, 0:512], trow2[0:1, :])
                v.wait_ge(sDv.h, sDv.n)
                v.tensor_tensor(stats2[0:1, 0:512], trow2[0:1, :], lrow[0:1, 0:512],
                                ALU.subtract)
                v.wait_ge(sDv.h, sDv.n)
                v.tensor_copy(lrow[0:1, 512:1024], stats2[0:1, 0:512])
                v.wait_ge(sDv.h, sDv.n)

            @blk.tensor
            def _(_eng, b4=b4):
                t.wait_ge(sDv.h, sDv.n)
                if b4 >= 1:
                    pass
                t.ldweights(onesb[0:1, :])
                t.matmul(pD[0][:, :], onesb[0:1, :], lrow[0:1, 0:512], start=True, stop=False)
                t.matmul(pD[0][:, :], onesb[0:1, :], lrow[0:1, 512:1024], start=False, stop=True)
                t.last.then_inc(sPe.inc(), 1)

            @blk.vector
            def _(_eng, b4=b4):
                v.wait_ge(sPe.h, sPe.n)
                v.tensor_copy(lrow[0:1, 0:512], trow[0:1, :])
                v.wait_ge(sDv.h, sDv.n)
                v.tensor_tensor(stats2[0:1, 0:512], trow[0:1, :], lrow[0:1, 0:512],
                                ALU.subtract)
                v.wait_ge(sDv.h, sDv.n)
                v.tensor_copy(lrow[0:1, 512:1024], stats2[0:1, 0:512])
                v.wait_ge(sDv.h, sDv.n)

            @blk.tensor
            def _(_eng, b4=b4):
                t.wait_ge(sDv.h, sDv.n)
                t.matmul(pD[1][:, :], onesb[0:1, :], lrow[0:1, 0:512], start=True, stop=False)
                t.matmul(pD[1][:, :], onesb[0:1, :], lrow[0:1, 512:1024], start=False, stop=True)
                t.last.then_inc(sPe.inc(), 1)

            if MK_DCUT < 2:
                continue
            for oc in range(32):
                @blk.vector
                def _(_eng, b4=b4, oc=oc):
                    if oc == 0:
                        v.wait_ge(sPe.h, sPe.n)
                    zt = zb[:, oc * 512:(oc + 1) * 512]
                    v.tensor_tensor(tq[oc % 2][:, :], zt, pD[0][:, :],
                                    ALU.mult)
                    v.wait_ge(sDv.h, sDv.n)
                    v.tensor_tensor(x1b[:, oc * 512:(oc + 1) * 512],
                                    tq[oc % 2][:, :], pD[1][:, :], ALU.subtract)
                    v.wait_ge(sDv.h, sDv.n)

            if MK_DCUT < 3:
                continue
            @blk.tensor
            def _(_eng, b4=b4):
                t.wait_ge(sDv.h, sDv.n)
                if b4 == 0:
                    t.wait_ge(sDw.h, sDw.n)
                for k in range(32):
                    t.ldweights(rwtb[:, k * 8:(k + 1) * 8])
                    t.matmul(pD[2][0:8, 0:512], rwtb[:, k * 8:(k + 1) * 8],
                             x1b[:, k * 512:(k + 1) * 512],
                             start=(k == 0), stop=(k == 31))
                t.last.then_inc(sPe.inc(), 1)

            @blk.vector
            def _(_eng, b4=b4):
                v.wait_ge(sPe.h, sPe.n)
                v.tensor_copy(l8[:, :], pD[2][0:8, 0:512])
                v.wait_ge(sDv.h, sDv.n)

            if MK_DCUT < 4:
                continue
            for q in range(4):
                @blk.tensor
                def _(_eng, b4=b4, q=q):
                    t.wait_ge(sDv.h, sDv.n)
                    t.transpose(pQ[0:128, 0:8], l8[:, q * 128:(q + 1) * 128], identb[0:8, 0:8])
                    t.last.then_inc(sPe.inc(), 1)

                @blk.vector
                def _(_eng, b4=b4, q=q):
                    v.wait_ge(sPe.h, sPe.n)
                    v.tensor_copy(lT[:, :], pQ[0:128, 0:8])
                    v.wait_ge(sDv.h, sDv.n)
                    v.reduce_max(mcol[:, :], lT[:, :], AX)
                    v.wait_ge(sDv.h, sDv.n)
                    v.tensor_scalar(rwq[:, :], lT[:, :], mcol[:, :], mcol[:, :],
                                    ALU.is_ge, ALU.mult)
                    v.wait_ge(sDv.h, sDv.n)

                @blk.tensor
                def _(_eng, b4=b4, q=q):
                    t.wait_ge(sDv.h, sDv.n)
                    t.transpose(pQ[0:8, 0:128], rwq[:, :], identb[:, :])
                    t.last.then_inc(sPe.inc(), 1)

                @blk.vector
                def _(_eng, b4=b4, q=q):
                    v.wait_ge(sPe.h, sPe.n)
                    v.tensor_copy(rwT[:, q * 128:(q + 1) * 128], pQ[0:8, 0:128])
                    v.wait_ge(sDv.h, sDv.n)

            if MK_DCUT < 5:
                continue
            @blk.tensor
            def _(_eng, b4=b4):
                t.wait_ge(sDv.h, sDv.n)
                t.ldweights(rep_s[0:8, :])
                t.matmul(pD[3][:, :], rep_s[0:8, :], rwT[:, :], start=True, stop=True)
                t.last.then_inc(sPe.inc(), 1)

            @blk.vector
            def _(_eng, b4=b4):
                v.wait_ge(sPe.h, sPe.n)
                v.tensor_copy(rw16[:, :], pD[3][:, :])
                v.wait_ge(sDv.h, sDv.n)

            @blk.tensor
            def _(_eng, b4=b4):
                t.wait_ge(sDv.h, sDv.n)
                for k in range(32):
                    t.ldweights(atb[:, k * 128:(k + 1) * 128])
                    t.matmul(pD[2][:, :], atb[:, k * 128:(k + 1) * 128],
                             x1b[:, k * 512:(k + 1) * 512],
                             start=(k == 0), stop=(k == 31))
                t.last.then_inc(sPe.inc(), 1)

            @blk.vector
            def _(_eng, b4=b4):
                v.wait_ge(sPe.h, sPe.n)
                v.tensor_tensor(whb[:, :], rw16[:, :], pD[2][:, :], ALU.mult)
                v.wait_ge(sDv.h, sDv.n)

            if MK_DCUT < 6:
                continue
            for oc in range(32):
                @blk.tensor
                def _(_eng, b4=b4, oc=oc):
                    if oc == 0:
                        t.wait_ge(sDv.h, sDv.n)
                    elif oc >= 2:
                        t.wait_ge(sDv.h, sDv.n)
                    t.ldweights(bwtb[:, oc * 128:(oc + 1) * 128])
                    t.matmul(pD[2 + oc % 2][:, :], bwtb[:, oc * 128:(oc + 1) * 128],
                             whb[:, :], start=True, stop=True)
                    t.last.then_inc(sPe.inc(), 1)

                @blk.vector
                def _(_eng, b4=b4, oc=oc):
                    v.wait_ge(sPe.h, sPe.n)
                    v.tensor_copy(ffnsb[:, oc * 512:(oc + 1) * 512], pD[2 + oc % 2][:, :])
                    v.wait_ge(sDv.h, sDv.n)

            if MK_DCUT < 7:
                continue
            @blk.vector
            def _(_eng, b4=b4):
                v.memset(stats2[:, :], 0.0)
                v.wait_ge(sDv.h, sDv.n)

            for oc in range(32):
                @blk.vector
                def _(_eng, b4=b4, oc=oc):
                    zt = zb[:, oc * 512:(oc + 1) * 512]
                    v.tensor_tensor(tq[0][:, :], zt, pD[0][:, :],
                                    ALU.mult)
                    v.wait_ge(sDv.h, sDv.n)
                    v.tensor_tensor(tq[1][:, :], tq[0][:, :], pD[1][:, :],
                                    ALU.subtract)
                    v.wait_ge(sDv.h, sDv.n)
                    v.tensor_add(zt, tq[1][:, :],
                                 ffnsb[:, oc * 512:(oc + 1) * 512])
                    v.wait_ge(sDv.h, sDv.n)
                    v.tensor_copy(zb2[:, :], zt)
                    v.wait_ge(sDv.h, sDv.n)

                @blk.scalar
                def _(_eng, b4=b4, oc=oc):
                    sc.wait_ge(sDv.h, sDv.n)
                    sc.activation(sq2[:, :], zb[:, oc * 512:(oc + 1) * 512], AF.Square)
                    sc.wait_ge(sAc.h, sAc.n)

                @blk.tensor
                def _(_eng, b4=b4, oc=oc):
                    t.wait_ge(sAc.h, sAc.n)
                    t.ldweights(onescol[:, 0:1])
                    t.matmul(pS2[0][:, :], onescol[:, 0:1], zb2[:, :], start=True, stop=True)
                    t.matmul(pS2[1][:, :], onescol[:, 0:1], sq2[:, :], start=True, stop=True)
                    t.last.then_inc(sPe.inc(), 1)

                @blk.vector
                def _(_eng, b4=b4, oc=oc):
                    v.wait_ge(sPe.h, sPe.n)
                    v.tensor_add(stats2[0:1, 0:512], stats2[0:1, 0:512], pS2[0][:, :])
                    v.tensor_add(stats2[0:1, 512:1024], stats2[0:1, 512:1024], pS2[1][:, :])
                    v.wait_ge(sDv.h, sDv.n)

            if MK_DCUT < 8:
                continue
            @blk.vector
            def _(_eng, b4=b4):
                v.tensor_scalar_mul(trow[0:1, :], stats2[0:1, 0:512], 1.0 / HID)
                v.tensor_scalar_mul(trow2[0:1, :], stats2[0:1, 512:1024], 1.0 / HID)
                v.wait_ge(sDv.h, sDv.n)
                v.tensor_tensor(stats2[0:1, 0:512], trow[0:1, :], trow[0:1, :], ALU.mult)
                v.wait_ge(sDv.h, sDv.n)
                v.tensor_tensor(trow2[0:1, :], trow2[0:1, :], stats2[0:1, 0:512], ALU.subtract)
                v.wait_ge(sDv.h, sDv.n)

            @blk.scalar
            def _(_eng, b4=b4):
                sc.wait_ge(sDv.h, sDv.n)
                sc.activation(trow2[0:1, :], trow2[0:1, :], AF.Ln, bias=EPS)
                sc.wait_ge(sAc.h, sAc.n)

            @blk.vector
            def _(_eng, b4=b4):
                v.wait_ge(sAc.h, sAc.n)
                v.tensor_scalar_mul(trow2[0:1, :], trow2[0:1, :], -0.5)
                v.wait_ge(sDv.h, sDv.n)

            @blk.scalar
            def _(_eng, b4=b4):
                sc.wait_ge(sDv.h, sDv.n)
                sc.activation(trow2[0:1, :], trow2[0:1, :], AF.Exp)
                sc.wait_ge(sAc.h, sAc.n)

            @blk.vector
            def _(_eng, b4=b4):
                v.wait_ge(sAc.h, sAc.n)
                v.tensor_mul(trow[0:1, :], trow[0:1, :], trow2[0:1, :])
                v.tensor_copy(lrow[0:1, 0:512], trow2[0:1, :])
                v.wait_ge(sDv.h, sDv.n)
                v.tensor_tensor(stats2[0:1, 0:512], trow2[0:1, :], lrow[0:1, 0:512], ALU.subtract)
                v.wait_ge(sDv.h, sDv.n)
                v.tensor_copy(lrow[0:1, 512:1024], stats2[0:1, 0:512])
                v.wait_ge(sDv.h, sDv.n)

            @blk.tensor
            def _(_eng, b4=b4):
                t.wait_ge(sDv.h, sDv.n)
                t.ldweights(onesb[0:1, :])
                t.matmul(pD[0][:, :], onesb[0:1, :], lrow[0:1, 0:512], start=True, stop=False)
                t.matmul(pD[0][:, :], onesb[0:1, :], lrow[0:1, 512:1024], start=False, stop=True)
                t.last.then_inc(sPe.inc(), 1)

            @blk.vector
            def _(_eng, b4=b4):
                v.wait_ge(sPe.h, sPe.n)
                v.tensor_copy(lrow[0:1, 0:512], trow[0:1, :])
                v.wait_ge(sDv.h, sDv.n)
                v.tensor_tensor(stats2[0:1, 0:512], trow[0:1, :], lrow[0:1, 0:512], ALU.subtract)
                v.wait_ge(sDv.h, sDv.n)
                v.tensor_copy(lrow[0:1, 512:1024], stats2[0:1, 0:512])
                v.wait_ge(sDv.h, sDv.n)

            @blk.tensor
            def _(_eng, b4=b4):
                t.wait_ge(sDv.h, sDv.n)
                t.matmul(pD[1][:, :], onesb[0:1, :], lrow[0:1, 0:512], start=True, stop=False)
                t.matmul(pD[1][:, :], onesb[0:1, :], lrow[0:1, 512:1024], start=False, stop=True)
                t.last.then_inc(sPe.inc(), 1)

            for oc in range(32):
                @blk.vector
                def _(_eng, b4=b4, oc=oc):
                    if oc == 0:
                        v.wait_ge(sPe.h, sPe.n)
                    if b4 > 0 or oc >= 2:
                        v.wait_ge(sOut.h, sOut.n)
                    zt = zb[:, oc * 512:(oc + 1) * 512]
                    v.tensor_tensor(tq[oc % 2][:, :], zt, pD[0][:, :],
                                    ALU.mult)
                    v.wait_ge(sDv.h, sDv.n)
                    v.tensor_tensor(ost[oc % 2][:, :], tq[oc % 2][:, :], pD[1][:, :],
                                    ALU.subtract)
                    v.wait_ge(sDv.h, sDv.n)

                @blk.sync
                def _(s, b4=b4, oc=oc):
                    s.wait_ge(sDv.h, sDv.n)
                    dst = AP(outT.tensor, oc * 128 * SEG + b4 * 512,
                             [[SEG, 128], [1, 512]])
                    s.dma_start(out=dst, in_=ost[oc % 2][:, :]).then_inc(sOut.inc(16), 16)

        @blk.sync
        def _(s):
            s.wait_ge(sOut.h, sOut.n)
            s.wait_ge(sAst[0].h, sAst[0].n)
            s.wait_ge(sAst[1].h, sAst[1].n)
            s.wait_ge(sCst[0].h, sCst[0].n)
            s.wait_ge(sCst[1].h, sCst[1].n)
            s.wait_ge(sSt.h, sSt.n)

    st.close()
    return nc


def _prep_inputs(inputs):
    x = np.asarray(inputs["L0_attn_hidden_states"])
    w1 = np.asarray(inputs["L0_attn_in_proj_w"])
    w2 = np.asarray(inputs["L0_attn_out_proj_w"])
    cw = np.asarray(inputs["L0_attn_conv_w"])
    cb = np.asarray(inputs["L0_attn_conv_b"])
    dtb = np.asarray(inputs["L0_attn_dt_bias"])
    al = np.asarray(inputs["L0_attn_A_log"])
    dd = np.asarray(inputs["L0_attn_D"])
    nw = np.asarray(inputs["L0_attn_norm_weight"])
    rw = np.asarray(inputs["L0_ffn_route_w"])
    aw = np.asarray(inputs["L0_ffn_A"])
    bw = np.asarray(inputs["L0_ffn_Bw"])

    w1t = np.ascontiguousarray(w1.reshape(JC_ALL, 128, HID).transpose(0, 2, 1))
    w2t = np.ascontiguousarray(w2.reshape(32, 128, INTER).transpose(0, 2, 1))
    convw = np.ascontiguousarray(
        cw.reshape(80, 128, 4).transpose(1, 0, 2).reshape(128, 320))
    convb = np.ascontiguousarray(cb.reshape(80, 128).T)
    dexp = np.empty((128, 64), np.float32)
    dexp[:64, :] = dd[0::2][None, :]
    dexp[64:, :] = dd[1::2][None, :]
    gam = np.ascontiguousarray(nw.reshape(64, 128).T)
    rwt = np.ascontiguousarray(rw.T)
    atw = np.ascontiguousarray(aw.reshape(128, HID).T)
    bwt = np.ascontiguousarray(bw.transpose(0, 2, 1).reshape(128, HID))

    in_maps = []
    for c in range(NC):
        b, s = divmod(c, 4)
        t0 = s * SEG - (HALO + 3)
        if t0 < 0:
            xs = np.concatenate([x[b, 0:HALO + 3], x[b, 0:SEG]], axis=0)
        else:
            xs = x[b, t0:t0 + SPAN]
        xTc = np.ascontiguousarray(xs.T)
        hm = np.full(128, 0.0 if s == 0 else 1.0, np.float32)
        in_maps.append(dict(
            xT=xTc, w1t=w1t, w2t=w2t, convw=convw, convb=convb,
            dtbias=dtb, alog=al, dexp=dexp, gam=gam, rwt=rwt, atw=atw,
            bwt=bwt, hmask=hm))
    return in_maps


def kernel(**inputs):
    if "nc" not in _CACHE:
        _CACHE["nc"] = build_nc()
    nc = _CACHE["nc"]
    in_maps = _prep_inputs(inputs)
    res = run_bass_kernel_spmd(nc, in_maps, core_ids=list(range(NC)))
    out = np.empty((2, 2 * SEG * 4 // 2, HID), np.float32)
    out = np.empty((2, 8192, HID), np.float32)
    for c in range(NC):
        b, s = divmod(c, 4)
        out[b, s * SEG:(s + 1) * SEG, :] = res.results[c]["outT"].T
    if DEBUG_TAPS:
        kernel.taps = res.results
    return out


# version 04 (final)
# speedup vs baseline: 1.3533x; 1.3533x over previous
import os
import numpy as np
from contextlib import ExitStack

import concourse.bass as bass
import concourse.mybir as mybir
from concourse.bass_utils import run_bass_kernel_spmd

F32 = mybir.dt.float32
BF16 = mybir.dt.bfloat16
AF = mybir.ActivationFunctionType
ALU = mybir.AluOpType
AX = mybir.AxisListType.X

HID = 4096
INTER = 8192
NC = 8
SEG = 2048
HALO = 512
SPAN = HALO + 3 + SEG
NCHUNK = (HALO + SEG) // 128
CI0 = HALO // 128
JC_ALL = 145
NBD = 81 * 128
EPS = 1e-5

DEBUG_TAPS = os.environ.get("MK_TAPS", "0") == "1"
MK_PASSES = os.environ.get("MK_PASSES", "ABCD")
MK_DCUT = int(os.environ.get("MK_DCUT", "9"))
_CACHE = {}


def AP(t, off, dims):
    return bass.AP(t, off, dims)


class Trk:
    def __init__(self, stack, nc, name):
        self.h = stack.enter_context(nc.semaphore(name))
        self.n = 0

    def inc(self, k=1):
        self.n += k
        return self.h



_DATAPATH = {
    "tensor_tensor", "tensor_scalar", "tensor_copy", "tensor_add", "tensor_sub",
    "tensor_mul", "tensor_max", "tensor_scalar_mul", "tensor_scalar_add",
    "tensor_single_scalar", "scalar_tensor_tensor", "tensor_tensor_scan",
    "tensor_reduce", "reduce_max", "reduce_sum", "reciprocal", "memset",
    "memzero", "iota", "activation", "select", "transpose",
}


class _WPE:

    def __init__(self, eng):
        self._e = eng
        self.last = None

    def __getattr__(self, name):
        f = getattr(self._e, name)
        if name in ("matmul", "transpose", "ldweights"):
            def g(*a, **k):
                ins = f(*a, **k)
                if name != "ldweights":
                    self.last = ins
                return ins
            return g
        return f


class _W:

    def __init__(self, eng, trk):
        self._e = eng
        self._t = trk

    def __getattr__(self, name):
        f = getattr(self._e, name)
        if name in _DATAPATH:
            trk = self._t
            eng = self._e

            def g(*a, **k):
                self._since = getattr(self, "_since", 0) + 1
                if self._since >= 4 and trk.n >= 1:
                    eng.wait_ge(trk.h, trk.n)
                    self._since = 0
                ins = f(*a, **k)
                ins.then_inc(trk.inc(), 1)
                return ins
            return g
        return f


def _finish(nc, blk, trks):
    @blk.sync
    def _(s):
        for t in trks:
            s.wait_ge(t.h, t.n)


def build_nc():
    nc = bass.Bass("TRN2", target_bir_lowering=False)

    xT = nc.dram_tensor("xT", [HID, SPAN], F32, kind="ExternalInput").ap()
    w1t = nc.dram_tensor("w1t", [JC_ALL, HID, 128], F32, kind="ExternalInput").ap()
    w2t = nc.dram_tensor("w2t", [32, INTER, 128], F32, kind="ExternalInput").ap()
    convw = nc.dram_tensor("convw", [128, 320], F32, kind="ExternalInput").ap()
    convb = nc.dram_tensor("convb", [128, 80], F32, kind="ExternalInput").ap()
    dtbias = nc.dram_tensor("dtbias", [128], F32, kind="ExternalInput").ap()
    alog = nc.dram_tensor("alog", [128], F32, kind="ExternalInput").ap()
    dexp = nc.dram_tensor("dexp", [128, 64], F32, kind="ExternalInput").ap()
    gam = nc.dram_tensor("gam", [128, 64], F32, kind="ExternalInput").ap()
    rwt = nc.dram_tensor("rwt", [HID, 8], F32, kind="ExternalInput").ap()
    atw = nc.dram_tensor("atw", [HID, 128], F32, kind="ExternalInput").ap()
    bwt = nc.dram_tensor("bwt", [128, HID], F32, kind="ExternalInput").ap()
    hmask = nc.dram_tensor("hmask", [128], F32, kind="ExternalInput").ap()

    outT = nc.dram_tensor("outT", [HID, SEG], F32, kind="ExternalOutput").ap()

    spk = "ExternalOutput" if DEBUG_TAPS else "Internal"
    proj_sp = nc.dram_tensor("proj_sp", [NBD, SPAN], BF16, kind=spk).ap()
    gate_sp = nc.dram_tensor("gate_sp", [INTER, SEG], BF16, kind=spk).ap()
    normed_sp = nc.dram_tensor("normed_sp", [INTER, SEG], BF16, kind=spk).ap()
    z_sp = nc.dram_tensor("z_sp", [HID, SEG], F32, kind=spk).ap()
    stg_d = nc.dram_tensor("stg_d", [2, 128, 128], BF16, kind="Internal").ap()
    stgL_d = nc.dram_tensor("stgL_d", [128], F32, kind="Internal").ap()
    rstd_d = nc.dram_tensor("rstd_d", [8, 128], BF16, kind="Internal").ap()

    st = ExitStack()
    e = st.enter_context

    def tk(name):
        return Trk(st, nc, name)

    sC = tk("sC")
    sGp = tk("sGp")
    sPe = tk("sPe")
    sDv = tk("sDv")
    sAc = tk("sAc")
    sSt = tk("sSt")
    sAx = tk("sAx")
    sAw = [tk("sAw0"), tk("sAw1"), tk("sAw2")]
    sAmm = tk("sAmm")
    sAev = tk("sAev")
    sAst = [tk("sAst0"), tk("sAst1")]
    sPj = [tk("sPj0"), tk("sPj1")]
    sGt = [tk("sGt0"), tk("sGt1")]
    sNw = tk("sNw")
    sCw = [tk("sCw0"), tk("sCw1"), tk("sCw2")]
    sCx = [tk("sCx0"), tk("sCx1")]
    sCst = [tk("sCst0"), tk("sCst1")]
    sZl = tk("sZl")
    sDw = tk("sDw")
    sOut = tk("sOut")

    convw_s = e(nc.sbuf_tensor("convw_s", [128, 320], F32))
    convb_s = e(nc.sbuf_tensor("convb_s", [128, 80], F32))
    dtb_s = e(nc.sbuf_tensor("dtb_s", [128, 1], F32))
    alog_s = e(nc.sbuf_tensor("alog_s", [128, 1], F32))
    negA_s = e(nc.sbuf_tensor("negA_s", [128, 1], F32))
    dexp_s = e(nc.sbuf_tensor("dexp_s", [128, 64], F32))
    gam_s = e(nc.sbuf_tensor("gam_s", [128, 64], F32))
    mask_s = e(nc.sbuf_tensor("mask_s", [128, 1], F32))
    onesb = e(nc.sbuf_tensor("onesb", [1, 128], BF16))
    onesf = e(nc.sbuf_tensor("onesf", [1, 128], F32))
    onescol = e(nc.sbuf_tensor("onescol", [128, 1], BF16))
    identb = e(nc.sbuf_tensor("identb", [128, 128], BF16))
    identf = e(nc.sbuf_tensor("identf", [128, 128], F32))
    trilC = e(nc.sbuf_tensor("trilC", [128, 128], BF16))
    zsel = e(nc.sbuf_tensor("zsel", [128, 16], BF16))
    rep_s = e(nc.sbuf_tensor("rep_s", [8, 128], BF16))
    epsc = e(nc.sbuf_tensor("epsc", [128, 1], F32))

    nc.const_aps.register(F32, EPS, epsc[:, :])

    v = _W(nc.vector, sDv)
    sc = _W(nc.scalar, sAc)
    g = _W(nc.gpsimd, sGp)
    t = _WPE(nc.tensor)

    blk = e(nc.Block())

    @blk.sync
    def _(s):
        s.dma_start(out=convw_s[:, :], in_=convw).then_inc(sC.inc(16), 16)
        s.dma_start(out=convb_s[:, :], in_=convb).then_inc(sC.inc(16), 16)
        s.dma_start(out=dtb_s[:, :], in_=dtbias.rearrange("(p o) -> p o", o=1)).then_inc(sC.inc(16), 16)
        s.dma_start(out=alog_s[:, :], in_=alog.rearrange("(p o) -> p o", o=1)).then_inc(sC.inc(16), 16)
        s.dma_start(out=dexp_s[:, :], in_=dexp).then_inc(sC.inc(16), 16)
        s.dma_start(out=gam_s[:, :], in_=gam).then_inc(sC.inc(16), 16)
        s.dma_start(out=mask_s[:, :], in_=hmask.rearrange("(p o) -> p o", o=1)).then_inc(sC.inc(16), 16)

    with ExitStack() as st0:
        it0 = st0.enter_context(nc.sbuf_tensor("it0", [128, 128], F32))
        it1 = st0.enter_context(nc.sbuf_tensor("it1", [128, 128], F32))
        tmpc_it = st0.enter_context(nc.sbuf_tensor("tmpc_it", [128, 128], F32))

        @blk.gpsimd
        def _(_eng):
            g.memset(epsc[:, :], EPS)
            g.memset(onesb[:, :], 1.0)
            g.memset(onesf[:, :], 1.0)
            g.memset(onescol[:, :], 1.0)
            g.memset(zsel[:, :], 0.0)
            g.wait_ge(sGp.h, sGp.n)
            g.memset(zsel[:, 8:9], 1.0)
            g.iota(it0[:, :], [[1, 128]], channel_multiplier=0,
                   allow_small_or_imprecise_dtypes=True)
            g.iota(it1[:, :], [[0, 128]], channel_multiplier=1,
                   allow_small_or_imprecise_dtypes=True)
            g.wait_ge(sGp.h, sGp.n)

        @blk.vector
        def _(_eng):
            v.wait_ge(sGp.h, sGp.n)
            v.tensor_tensor(identf[:, :], it0[:, :], it1[:, :], ALU.is_equal)
            v.wait_ge(sDv.h, sDv.n)
            v.tensor_copy(identb[:, :], identf[:, :])
            v.tensor_tensor(trilC[:, :], it0[:, :], it1[:, :], ALU.is_ge)
            v.wait_ge(sDv.h, sDv.n)

        @blk.gpsimd
        def _(_eng):
            g.wait_ge(sDv.h, sDv.n)
            g.iota(it1[:, :], [[1, 128]], channel_multiplier=-16,
                   allow_small_or_imprecise_dtypes=True)
            g.wait_ge(sGp.h, sGp.n)

        @blk.vector
        def _(_eng):
            v.wait_ge(sGp.h, sGp.n)
            v.tensor_scalar(it0[:, :], it1[:, :], 0.0, None, ALU.is_ge)
            v.tensor_scalar(tmpc_it[:, :], it1[:, :], 15.0, None, ALU.is_le)
            v.wait_ge(sDv.h, sDv.n)
            v.tensor_mul(it0[:, :], it0[:, :], tmpc_it[:, :])
            v.wait_ge(sDv.h, sDv.n)
            v.tensor_copy(rep_s[:, :], it0[0:8, :])
            v.wait_ge(sDv.h, sDv.n)

        @blk.scalar
        def _(_eng):
            sc.wait_ge(sC.h, sC.n)
            sc.wait_ge(sDv.h, sDv.n)
            sc.activation(negA_s[:, :], alog_s[:, :], AF.Exp)
            sc.wait_ge(sAc.h, sAc.n)

        @blk.vector
        def _(_eng):
            v.wait_ge(sAc.h, sAc.n)
            v.tensor_scalar_mul(negA_s[:, :], negA_s[:, :], -1.0)
            v.wait_ge(sDv.h, sDv.n)

    with ExitStack() as stA:
        eA = stA.enter_context
        xtb = eA(nc.sbuf_tensor("xtb", [128, 32 * SPAN], BF16))
        wbufA = [eA(nc.sbuf_tensor(f"wbufA{i}", [128, 32 * 128], BF16)) for i in range(3)]
        stagA = eA(nc.sbuf_tensor("stagA", [128, 12 * 512], BF16))
        pA = [eA(nc.psum_tensor(f"pA{i}", [128, 512], F32)) for i in range(6)]

        @blk.gpsimd
        def _(_eng):
            g.wait_ge(sDv.h, sDv.n)
            g.wait_ge(sAc.h, sAc.n)
            for i in range(8):
                src = AP(xT.tensor, i * 4 * 128 * SPAN,
                         [[SPAN, 128], [128 * SPAN, 4], [1, SPAN]])
                dst = AP(xtb, i * 4 * SPAN, [[32 * SPAN, 128], [SPAN, 4], [1, SPAN]])
                g.dma_start(out=dst, in_=src).then_inc(sAx.inc(16), 16)

        jc_order = list(range(64, 145)) + list(range(0, 64))

        def ntiles_for(jc):
            if jc >= 64:
                return [(0, 512), (512, 512), (1024, 512), (1536, 512),
                        (2048, 512), (2560, 3)]
            return [(515, 512), (1027, 512), (1539, 512), (2051, 512)]

        for ji, jc in enumerate(jc_order):
            par = ji % 3
            nts = ntiles_for(jc)

            @blk.gpsimd
            def _(_eng, ji=ji, jc=jc, par=par):
                if ji >= 3:
                    g.wait_ge(sAmm.h, ji - 2)
                src = AP(w1t.tensor, jc * HID * 128,
                         [[128, 128], [128 * 128, 32], [1, 128]])
                dst = AP(wbufA[par], 0, [[32 * 128, 128], [128, 32], [1, 128]])
                g.dma_start(out=dst, in_=src).then_inc(sAw[par].inc(16), 16)

            @blk.tensor
            def _(_eng, ji=ji, jc=jc, par=par, nts=nts):
                if ji == 0:
                    t.wait_ge(sAx.h, sAx.n)
                t.wait_ge(sAw[par].h, sAw[par].n)
                if ji >= 1:
                    t.wait_ge(sDv.h, sDv.n)
                for k in range(32):
                    t.ldweights(wbufA[par][:, k * 128:(k + 1) * 128])
                    for b, (t0, nn) in enumerate(nts):
                        t.matmul(pA[b][:, 0:nn],
                                 wbufA[par][:, k * 128:(k + 1) * 128],
                                 xtb[:, k * SPAN + t0:k * SPAN + t0 + nn],
                                 start=(k == 0), stop=(k == 31))
                t.last.then_inc(sAmm.inc(), 1)

            @blk.vector
            def _(_eng, ji=ji, nts=nts):
                v.wait_ge(sAmm.h, ji + 1)
                if ji >= 2:
                    v.wait_ge(sAst[ji % 2].h, sAst[ji % 2].n)
                sb = (ji % 2) * 6
                for b, (t0, nn) in enumerate(nts):
                    v.tensor_copy(stagA[:, (sb + b) * 512:(sb + b) * 512 + nn],
                                  pA[b][:, 0:nn])
                v.nop().then_inc(sAev.inc(), 1)

            @blk.sync
            def _(s, ji=ji, jc=jc, nts=nts):
                s.wait_ge(sDv.h, sDv.n)
                sb = (ji % 2) * 6
                for b, (t0, nn) in enumerate(nts):
                    if jc >= 64:
                        dst = AP(proj_sp.tensor, (jc - 64) * 128 * SPAN + t0,
                                 [[SPAN, 128], [1, nn]])
                    else:
                        dst = AP(gate_sp.tensor, jc * 128 * SEG + (t0 - 515),
                                 [[SEG, 128], [1, nn]])
                    s.dma_start(out=dst,
                                in_=AP(stagA, (sb + b) * 512, [[12 * 512, 128], [1, nn]])
                                ).then_inc(sAst[ji % 2].inc(16), 16)

    if "B" not in MK_PASSES:
        _finish(nc, blk, [sAst[0], sAst[1], sSt, sOut])
        st.close()
        return nc
    with ExitStack() as stB:
        eB = stB.enter_context
        projb = eB(nc.sbuf_tensor("projb", [128, 81 * 131], BF16))
        gateb = eB(nc.sbuf_tensor("gateb", [128, INTER], BF16))
        cacc = eB(nc.sbuf_tensor("cacc", [128, 16 * 128], F32))
        tcv = eB(nc.sbuf_tensor("tcv", [128, 8 * 128], F32))
        hsF = eB(nc.sbuf_tensor("hsF", [128, INTER], BF16))
        BCF = eB(nc.sbuf_tensor("BCF", [128, 2048], BF16))
        hs_sp = eB(nc.sbuf_tensor("hs_sp", [128, INTER], BF16))
        B_sp = eB(nc.sbuf_tensor("B_sp", [128, 1024], BF16))
        Xd = eB(nc.sbuf_tensor("Xd", [128, INTER], BF16))
        stateF = eB(nc.sbuf_tensor("stateF", [128, INTER], F32))
        state_b = eB(nc.sbuf_tensor("state_b", [128, INTER], BF16))
        dt_f = eB(nc.sbuf_tensor("dt_f", [128, 128], F32))
        dtT = eB(nc.sbuf_tensor("dtT", [128, 128], F32))
        acs = eB(nc.sbuf_tensor("acs", [128, 128], F32))
        acsT = eB(nc.sbuf_tensor("acsT", [128, 128], F32))
        hi_t = eB(nc.sbuf_tensor("hi_t", [128, 128], BF16))
        lo_t = eB(nc.sbuf_tensor("lo_t", [128, 128], BF16))
        shalf = [eB(nc.sbuf_tensor(f"shalf{i}", [1, 4096], BF16)) for i in range(2)]
        stgLrow = eB(nc.sbuf_tensor("stgLrow", [1, 128], F32))
        ddT = eB(nc.sbuf_tensor("ddT", [128, 128], F32))
        dcrep = eB(nc.sbuf_tensor("dcrep", [128, 128], F32))
        tmpa = eB(nc.sbuf_tensor("tmpa", [128, 128], F32))
        tmpb = eB(nc.sbuf_tensor("tmpb", [128, 128], F32))
        cbt = eB(nc.sbuf_tensor("cbt", [128, 8 * 128], F32))
        t1t = eB(nc.sbuf_tensor("t1t", [128, 8 * 128], F32))
        ebuf = eB(nc.sbuf_tensor("ebuf", [128, 8 * 128], BF16))
        s1buf = eB(nc.sbuf_tensor("s1buf", [128, 8 * 128], BF16))
        mful = [eB(nc.sbuf_tensor(f"mful{i}", [128, 8 * 128], BF16)) for i in range(2)]
        csdo = [eB(nc.sbuf_tensor(f"csdo{i}", [128, 8 * 128], BF16)) for i in range(2)]
        tpair = eB(nc.sbuf_tensor("tpair", [128, 4 * 128], F32))
        gsil = eB(nc.sbuf_tensor("gsil", [128, 8 * 128], BF16))
        rstdg = eB(nc.sbuf_tensor("rstdg", [8, 128], F32))
        rstdg2 = eB(nc.sbuf_tensor("rstdg2", [8, 128], F32))
        rstdgb = eB(nc.sbuf_tensor("rstdgb", [8, 128], BF16))
        rrow = eB(nc.sbuf_tensor("rrow", [1, 1024], BF16))
        nstag = eB(nc.sbuf_tensor("nstag", [128, 8 * 128], BF16))

        pT = [eB(nc.psum_tensor(f"pT{i}", [128, 1024], BF16)) for i in range(2)]
        pF = [eB(nc.psum_tensor(f"pF{i}", [128, 512], F32)) for i in range(2)]
        pLa = eB(nc.psum_tensor("pLa", [128, 512], F32))
        pLb = eB(nc.psum_tensor("pLb", [128, 512], F32))
        pY = [eB(nc.psum_tensor(f"pY{i}", [128, 512], F32)) for i in range(2)]

        @blk.vector
        def _(_eng):
            v.memset(stateF[:, :], 0.0)
            v.memset(state_b[:, :], 0.0)
            v.wait_ge(sDv.h, sDv.n)

        conv_order = list(range(64, 80)) + list(range(0, 64))
        gpConv = {}
        dvDt = {}

        for ci in range(NCHUNK):
            real = ci >= CI0
            par = ci % 2

            @blk.sync
            def _(s, ci=ci, real=real, par=par):
                if ci == 0:
                    s.wait_ge(sAst[0].h, sAst[0].n)
                    s.wait_ge(sAst[1].h, sAst[1].n)
                if ci >= 1:
                    s.wait_ge(sGp.h, gpConv[ci - 1])
                    s.wait_ge(sDv.h, dvDt[ci - 1])
                src = AP(proj_sp.tensor, ci * 128,
                         [[SPAN, 128], [128 * SPAN, 81], [1, 131]])
                dst = AP(projb, 0, [[81 * 131, 128], [131, 81], [1, 131]])
                s.dma_start(out=dst, in_=src).then_inc(sPj[par].inc(16), 16)

            if real:
                @blk.sync
                def _(s, ci=ci, par=par):
                    s.wait_ge(sDv.h, sDv.n)
                    srcg = AP(gate_sp.tensor, (ci - CI0) * 128,
                              [[SEG, 128], [128 * SEG, 64], [1, 128]])
                    dstg = AP(gateb, 0, [[INTER, 128], [128, 64], [1, 128]])
                    s.dma_start(out=dstg, in_=srcg).then_inc(sGt[par].inc(16), 16)

            @blk.vector
            def _(_eng, ci=ci, par=par):
                v.wait_ge(sPj[par].h, sPj[par].n)
                if ci == CI0:
                    m3 = AP(projb, 0, [[81 * 131, 128], [131, 81], [1, 3]])
                    v.tensor_scalar(m3, m3, mask_s[:, :], None, ALU.mult)
                dts = 80 * 131
                v.tensor_scalar(tmpa[:, :], projb[:, dts + 3:dts + 131],
                                dtb_s[:, :], None, ALU.add)
                v.wait_ge(sDv.h, sDv.n)
            dvDt[ci] = sDv.n

            @blk.scalar
            def _(_eng, ci=ci):
                sc.wait_ge(sDv.h, sDv.n)
                sc.activation(tmpb[:, :], tmpa[:, :], AF.Abs)
                sc.wait_ge(sAc.h, sAc.n)
                sc.activation(tmpb[:, :], tmpb[:, :], AF.Exp, scale=-1.0)
                sc.wait_ge(sAc.h, sAc.n)
                sc.activation(tmpb[:, :], tmpb[:, :], AF.Ln, bias=1.0)
                sc.wait_ge(sAc.h, sAc.n)

            @blk.vector
            def _(_eng, ci=ci):
                v.wait_ge(sAc.h, sAc.n)
                v.tensor_scalar(dt_f[:, :], tmpa[:, :], 0.0, None, ALU.max)
                v.wait_ge(sDv.h, sDv.n)
                v.tensor_add(dt_f[:, :], dt_f[:, :], tmpb[:, :])
                v.wait_ge(sDv.h, sDv.n)
                v.tensor_scalar(tmpa[:, :], dt_f[:, :], negA_s[:, :], None, ALU.mult)
                v.wait_ge(sDv.h, sDv.n)
                v.tensor_tensor_scan(acs[:, :], tmpa[:, :], tmpa[:, :], 0.0,
                                     ALU.add, ALU.bypass)
                v.wait_ge(sDv.h, sDv.n)
                v.tensor_copy(hi_t[:, :], acs[:, :])
                v.wait_ge(sDv.h, sDv.n)
                v.tensor_tensor(lo_t[:, :], acs[:, :], hi_t[:, :], ALU.subtract)
                v.wait_ge(sDv.h, sDv.n)

            @blk.tensor
            def _(_eng, ci=ci):
                t.wait_ge(sDv.h, sDv.n)
                t.transpose(pF[0][:, 0:128], acs[:, :], identf[:, :])
                t.transpose(pF[0][:, 128:256], dt_f[:, :], identf[:, :])
                t.last.then_inc(sPe.inc(), 1)

            @blk.vector
            def _(_eng, ci=ci):
                v.wait_ge(sPe.h, sPe.n)
                v.tensor_copy(acsT[:, :], pF[0][:, 0:128])
                v.tensor_copy(dtT[:, :], pF[0][:, 128:256])
                v.wait_ge(sDv.h, sDv.n)

            @blk.sync
            def _(s, ci=ci):
                s.wait_ge(sDv.h, sDv.n)
                s.wait_ge(sSt.h, sSt.n)
                s.dma_start(out=stg_d[0], in_=hi_t[:, :]).then_inc(sSt.inc(16), 16)
                s.dma_start(out=stg_d[1], in_=lo_t[:, :]).then_inc(sSt.inc(16), 16)
                s.dma_start(out=stgL_d, in_=acs[:, 127:128]).then_inc(sSt.inc(16), 16)
                s.wait_ge(sSt.h, sSt.n)
                s.dma_start(out=shalf[0][0:1, :],
                            in_=AP(stg_d.tensor, 0, [[16384, 1], [1, 4096]]),
                            single_packet=True).then_inc(sSt.inc(16), 16)
                s.dma_start(out=shalf[1][0:1, :],
                            in_=AP(stg_d.tensor, 16384, [[16384, 1], [1, 4096]]),
                            single_packet=True).then_inc(sSt.inc(16), 16)
                s.dma_start(out=stgLrow[0:1, :],
                            in_=AP(stgL_d.tensor, 0, [[128, 1], [1, 128]]),
                            single_packet=True).then_inc(sSt.inc(16), 16)

            @blk.tensor
            def _(_eng, ci=ci):
                t.wait_ge(sSt.h, sSt.n)
                t.matmul(pF[1][:, 0:128], onesf[0:1, :], stgLrow[0:1, :],
                         start=True, stop=True)
                t.last.then_inc(sPe.inc(), 1)

            @blk.vector
            def _(_eng, ci=ci):
                v.wait_ge(sPe.h, sPe.n)
                v.tensor_tensor(tmpb[:, :], pF[1][:, 0:128], acsT[:, :], ALU.subtract)
                v.wait_ge(sDv.h, sDv.n)

            @blk.scalar
            def _(_eng, ci=ci):
                sc.wait_ge(sDv.h, sDv.n)
                sc.activation(dcrep[:, :], pF[1][:, 0:128], AF.Exp)
                sc.activation(tmpa[:, :], tmpb[:, :], AF.Exp)
                sc.wait_ge(sAc.h, sAc.n)

            @blk.vector
            def _(_eng, ci=ci):
                v.wait_ge(sAc.h, sAc.n)
                v.tensor_mul(ddT[:, :], tmpa[:, :], dtT[:, :])
                v.wait_ge(sDv.h, sDv.n)

            silu_done = []
            for wv in range(10):
                tiles = conv_order[wv * 8:(wv + 1) * 8]

                @blk.gpsimd
                def _(_eng, ci=ci, par=par, wv=wv, tiles=tiles, silu_done=silu_done):
                    if wv == 0:
                        g.wait_ge(sPj[par].h, sPj[par].n)
                        g.wait_ge(sDv.h, sDv.n)
                    if wv >= 2:
                        g.wait_ge(sAc.h, silu_done[wv - 2])
                    jt0 = tiles[0]
                    acc8 = AP(cacc, (wv % 2) * 8 * 128, [[16 * 128, 128], [1, 1024]])

                    def pslice(k):
                        return AP(projb, jt0 * 131 + k,
                                  [[81 * 131, 128], [131, 8], [1, 128]])

                    def wb8(k):
                        return AP(convw_s, 4 * jt0 + k, [[320, 128], [4, 8], [0, 128]])
                    g.tensor_tensor(acc8, pslice(0), wb8(0), ALU.mult)
                    for k in range(1, 4):
                        g.wait_ge(sGp.h, sGp.n)
                        g.tensor_tensor(tcv[:, :], pslice(k), wb8(k), ALU.mult)
                        g.wait_ge(sGp.h, sGp.n)
                        g.tensor_tensor(acc8, acc8, tcv[:, :], ALU.add)
                    g.wait_ge(sGp.h, sGp.n)

                @blk.scalar
                def _(_eng, ci=ci, wv=wv, tiles=tiles):
                    sc.wait_ge(sGp.h, sGp.n)
                    if wv == 0 and ci >= 1:
                        sc.wait_ge(sPe.h, sPe.n)
                    for i, jt in enumerate(tiles):
                        sl = (wv % 2) * 8 + i
                        a = cacc[:, sl * 128:(sl + 1) * 128]
                        if jt < 64:
                            dst = hsF[:, jt * 128:(jt + 1) * 128]
                        else:
                            dst = BCF[:, (jt - 64) * 128:(jt - 63) * 128]
                        sc.activation(dst, a, AF.Silu, bias=convb_s[:, jt:jt + 1])
                    sc.wait_ge(sAc.h, sAc.n)
                silu_done.append(sAc.n)
                if wv == 9:
                    gpConv[ci] = sGp.n

            for wv in range(9):
                @blk.tensor
                def _(_eng, ci=ci, wv=wv, silu_done=silu_done):
                    if wv == 0:
                        t.wait_ge(sAc.h, silu_done[0])
                        t.wait_ge(sDv.h, sDv.n)
                    else:
                        t.wait_ge(sAc.h, silu_done[min(wv + 1, 9)])
                    if wv >= 2:
                        t.wait_ge(sDv.h, sDv.n)
                    pb = pT[wv % 2]
                    if wv == 0:
                        for i in range(8):
                            t.transpose(pb[:, i * 128:(i + 1) * 128],
                                        BCF[:, i * 128:(i + 1) * 128], identb[:, :])
                    else:
                        for i in range(8):
                            jt = (wv - 1) * 8 + i
                            t.transpose(pb[:, i * 128:(i + 1) * 128],
                                        hsF[:, jt * 128:(jt + 1) * 128], identb[:, :])
                    t.last.then_inc(sPe.inc(), 1)

                @blk.vector
                def _(_eng, ci=ci, wv=wv):
                    v.wait_ge(sPe.h, sPe.n)
                    pb = pT[wv % 2]
                    if wv == 0:
                        v.tensor_copy(B_sp[:, :], pb[:, :])
                    else:
                        v.tensor_copy(hs_sp[:, (wv - 1) * 1024:wv * 1024], pb[:, :])
                    v.wait_ge(sDv.h, sDv.n)

            @blk.vector
            def _(_eng, ci=ci):
                v.wait_ge(sDv.h, sDv.n)
                dd_ap = AP(ddT, 0, [[128, 128], [1, 128], [0, 64]])
                v.tensor_tensor(Xd[:, :], hs_sp[:, :], dd_ap, ALU.mult)
                v.wait_ge(sDv.h, sDv.n)

            if real:
                if ci == CI0:
                    @blk.vector
                    def _(_eng):
                        v.tensor_scalar(stateF[:, :], stateF[:, :], mask_s[:, :],
                                        None, ALU.mult)
                        v.tensor_scalar(state_b[:, :], state_b[:, :], mask_s[:, :],
                                        None, ALU.mult)
                        v.wait_ge(sDv.h, sDv.n)

                @blk.tensor
                def _(_eng, ci=ci):
                    t.wait_ge(sDv.h, sDv.n)
                    for g8 in range(8):
                        pb = pF[g8 // 4]
                        off = (g8 % 4) * 128
                        t.ldweights(BCF[:, g8 * 128:(g8 + 1) * 128])
                        t.matmul(pb[:, off:off + 128],
                                 BCF[:, g8 * 128:(g8 + 1) * 128],
                                 BCF[:, (8 + g8) * 128:(9 + g8) * 128],
                                 start=(g8 % 4 == 0), stop=(g8 % 4 == 3))
                    t.last.then_inc(sPe.inc(), 1)

                @blk.vector
                def _(_eng, ci=ci):
                    v.wait_ge(sPe.h, sPe.n)
                    for g8 in range(8):
                        pb = pF[g8 // 4]
                        off = (g8 % 4) * 128
                        v.tensor_tensor(cbt[:, g8 * 128:(g8 + 1) * 128],
                                        pb[:, off:off + 128], trilC[:, :], ALU.mult)
                    v.wait_ge(sDv.h, sDv.n)

                peP1, peP2, dvT1, dvM, acE, acGs, dvYe = {}, {}, {}, {}, {}, {}, {}
                for w in range(17):
                    if w in (4, 8, 12):
                        @blk.sync
                        def _(s, ci=ci, w=w):
                            s.wait_ge(sPe.h, sPe.n)
                            q = (w // 4) * 4096
                            s.dma_start(out=shalf[0][0:1, :],
                                        in_=AP(stg_d.tensor, q, [[16384, 1], [1, 4096]]),
                                        single_packet=True).then_inc(sSt.inc(16), 16)
                            s.dma_start(out=shalf[1][0:1, :],
                                        in_=AP(stg_d.tensor, 16384 + q, [[16384, 1], [1, 4096]]),
                                        single_packet=True).then_inc(sSt.inc(16), 16)

                    if w < 16:
                        @blk.tensor
                        def _(_eng, ci=ci, w=w, dv0=sDv.n):
                            if w >= 2:
                                t.wait_ge(sDv.h, dvT1[w - 2])
                                t.wait_ge(sAc.h, acE[w - 2])
                            else:
                                t.wait_ge(sDv.h, dv0)
                            if w in (4, 8, 12):
                                t.wait_ge(sSt.h, sSt.n)
                            pba = pLa if w % 2 == 0 else pF[0]
                            pbb = pLb if w % 2 == 0 else pF[1]
                            t.ldweights(onesb[0:1, :])
                            for i in range(8):
                                h = 8 * w + i
                                pb = pba if i < 4 else pbb
                                off = (i % 4) * 128
                                hh = h % 32
                                t.matmul(pb[:, off:off + 128], onesb[0:1, :],
                                         shalf[0][0:1, hh * 128:(hh + 1) * 128],
                                         start=(i % 4 == 0), stop=False)
                                t.matmul(pb[:, off:off + 128], onesb[0:1, :],
                                         shalf[1][0:1, hh * 128:(hh + 1) * 128],
                                         start=False, stop=(i % 4 == 3))
                            t.last.then_inc(sPe.inc(), 1)
                            peP1[w] = sPe.n

                        @blk.vector
                        def _(_eng, ci=ci, w=w):
                            v.wait_ge(sPe.h, peP1[w])
                            pba = pLa if w % 2 == 0 else pF[0]
                            pbb = pLb if w % 2 == 0 else pF[1]
                            for i in range(8):
                                h = 8 * w + i
                                pb = pba if i < 4 else pbb
                                off = (i % 4) * 128
                                v.tensor_scalar(t1t[:, i * 128:(i + 1) * 128],
                                                pb[:, off:off + 128],
                                                acsT[:, h:h + 1], 87.0,
                                                ALU.subtract, ALU.min)
                            dvT1[w] = sDv.n

                        @blk.scalar
                        def _(_eng, ci=ci, w=w):
                            sc.wait_ge(sDv.h, dvT1[w])
                            pba = pLa if w % 2 == 0 else pF[0]
                            pbb = pLb if w % 2 == 0 else pF[1]
                            sc.activation(ebuf[:, :], t1t[:, :], AF.Exp)
                            sc.activation(s1buf[:, 0:512], pba[:, :], AF.Exp)
                            sc.activation(s1buf[:, 512:1024], pbb[:, :], AF.Exp)
                            acE[w] = sAc.n

                        @blk.vector
                        def _(_eng, ci=ci, w=w):
                            v.wait_ge(sAc.h, acE[w])
                            if w >= 2:
                                v.wait_ge(sPe.h, peP2.get(w - 2, 0))
                            wp = w % 2
                            for i in range(8):
                                h = 8 * w + i
                                v.scalar_tensor_tensor(
                                    mful[wp][:, i * 128:(i + 1) * 128],
                                    cbt[:, i * 128:(i + 1) * 128],
                                    dtT[:, h:h + 1],
                                    ebuf[:, i * 128:(i + 1) * 128],
                                    ALU.mult, ALU.mult)
                            v.tensor_tensor(csdo[wp][:, :], BCF[:, 1024:2048],
                                            s1buf[:, :], ALU.mult)
                            dvM[w] = sDv.n

                    if w >= 1:
                        @blk.tensor
                        def _(_eng, ci=ci, w=w):
                            t.wait_ge(sDv.h, dvM[w - 1])
                            if w >= 3:
                                t.wait_ge(sDv.h, dvYe[w - 3])
                            wp = (w - 1) % 2
                            pyb = pY[wp]
                            for p4 in range(4):
                                pg = 4 * (w - 1) + p4
                                for sub in range(2):
                                    h = 2 * pg + sub
                                    i = h - 8 * (w - 1)
                                    t.ldweights(hs_sp[:, h * 64:(h + 1) * 64])
                                    t.matmul(pyb[64 * sub:64 * (sub + 1), p4 * 128:(p4 + 1) * 128],
                                             hs_sp[:, h * 64:(h + 1) * 64],
                                             mful[wp][:, i * 128:(i + 1) * 128],
                                             start=(p4 == 0), stop=False,
                                             skip_group_check=True)
                                    t.ldweights(state_b[:, h * 64:(h + 1) * 64])
                                    t.matmul(pyb[64 * sub:64 * (sub + 1), p4 * 128:(p4 + 1) * 128],
                                             state_b[:, h * 64:(h + 1) * 64],
                                             csdo[wp][:, i * 128:(i + 1) * 128],
                                             start=False,
                                             stop=(p4 == 3),
                                             skip_group_check=True)
                            t.last.then_inc(sPe.inc(), 1)
                            peP2[w - 1] = sPe.n

                        @blk.scalar
                        def _(_eng, ci=ci, w=w, par=par):
                            if w == 1:
                                sc.wait_ge(sGt[par].h, sGt[par].n)
                            if w >= 3:
                                sc.wait_ge(sDv.h, dvYe[w - 3])
                            gp2 = ((w - 1) % 2) * 512
                            for p4 in range(4):
                                pg = 4 * (w - 1) + p4
                                sc.activation(gsil[:, gp2 + p4 * 128:gp2 + (p4 + 1) * 128],
                                              gateb[:, pg * 128:(pg + 1) * 128],
                                              AF.Silu)
                            acGs[w - 1] = sAc.n

                        @blk.vector
                        def _(_eng, ci=ci, w=w):
                            v.wait_ge(sPe.h, peP2[w - 1])
                            v.wait_ge(sAc.h, acGs[w - 1])
                            wp = (w - 1) % 2
                            gp2 = wp * 512
                            pyb = pY[wp]
                            for p4 in range(4):
                                pg = 4 * (w - 1) + p4
                                v.scalar_tensor_tensor(
                                    tpair[:, p4 * 128:(p4 + 1) * 128],
                                    hsF[:, pg * 128:(pg + 1) * 128],
                                    dexp_s[:, pg:pg + 1],
                                    pyb[:, p4 * 128:(p4 + 1) * 128],
                                    ALU.mult, ALU.add)
                            v.wait_ge(sDv.h, sDv.n)
                            for p4 in range(4):
                                pg = 4 * (w - 1) + p4
                                v.tensor_tensor(gateb[:, pg * 128:(pg + 1) * 128],
                                                tpair[:, p4 * 128:(p4 + 1) * 128],
                                                gsil[:, gp2 + p4 * 128:gp2 + (p4 + 1) * 128],
                                                ALU.mult)
                            dvYe[w - 1] = sDv.n

            @blk.vector
            def _(_eng, ci=ci):
                dc_ap = AP(dcrep, 0, [[128, 128], [1, 128], [0, 64]])
                v.tensor_tensor(stateF[:, :], stateF[:, :], dc_ap, ALU.mult)
                v.wait_ge(sDv.h, sDv.n)

            for g8 in range(8):
                @blk.tensor
                def _(_eng, ci=ci, g8=g8):
                    if g8 == 0:
                        t.wait_ge(sDv.h, sDv.n)
                        t.wait_ge(sAc.h, sAc.n)
                    elif g8 >= 2:
                        t.wait_ge(sDv.h, sDv.n)
                    pb = pF[g8 % 2]
                    t.ldweights(B_sp[:, g8 * 128:(g8 + 1) * 128])
                    for i in range(8):
                        h = g8 + 8 * i
                        t.matmul(pb[:, i * 64:(i + 1) * 64],
                                 B_sp[:, g8 * 128:(g8 + 1) * 128],
                                 Xd[:, h * 64:(h + 1) * 64],
                                 start=(i == 0), stop=(i == 7))
                    t.last.then_inc(sPe.inc(), 1)

                @blk.vector
                def _(_eng, ci=ci, g8=g8):
                    v.wait_ge(sPe.h, sPe.n)
                    pb = pF[g8 % 2]
                    sl = AP(stateF, g8 * 64, [[INTER, 128], [512, 8], [1, 64]])
                    v.tensor_add(sl, sl, pb[:, :])
                    v.wait_ge(sDv.h, sDv.n)

            @blk.vector
            def _(_eng, ci=ci):
                v.wait_ge(sDv.h, sDv.n)
                v.tensor_copy(state_b[:, :], stateF[:, :])
                v.wait_ge(sDv.h, sDv.n)

            if real:
                for wv in range(8):
                    @blk.gpsimd
                    def _(_eng, ci=ci, wv=wv):
                        g.wait_ge(sDv.h, sDv.n)
                        if wv >= 1:
                            g.wait_ge(sPe.h, sPe.n)
                        for i in range(8):
                            j = wv * 8 + i
                            g.tensor_mul(nstag[:, i * 128:(i + 1) * 128],
                                         gateb[:, j * 128:(j + 1) * 128],
                                         gateb[:, j * 128:(j + 1) * 128])
                        g.wait_ge(sGp.h, sGp.n)

                    @blk.tensor
                    def _(_eng, ci=ci, wv=wv):
                        t.wait_ge(sGp.h, sGp.n)
                        t.ldweights(zsel[:, (8 - wv):(16 - wv)])
                        for i in range(8):
                            t.matmul(pF[0][0:8, 0:128],
                                     zsel[:, (8 - wv):(16 - wv)],
                                     nstag[:, i * 128:(i + 1) * 128],
                                     start=(wv == 0 and i == 0),
                                     stop=(wv == 7 and i == 7))
                        t.last.then_inc(sPe.inc(), 1)

                @blk.vector
                def _(_eng, ci=ci):
                    v.wait_ge(sPe.h, sPe.n)
                    v.tensor_scalar_mul(rstdg[:, :], pF[0][0:8, 0:128], 1.0 / 1024.0)
                    v.wait_ge(sDv.h, sDv.n)

                @blk.scalar
                def _(_eng, ci=ci):
                    sc.wait_ge(sDv.h, sDv.n)
                    sc.activation(rstdg2[:, :], rstdg[:, :], AF.Ln, bias=EPS)
                    sc.wait_ge(sAc.h, sAc.n)

                @blk.vector
                def _(_eng, ci=ci):
                    v.wait_ge(sAc.h, sAc.n)
                    v.tensor_scalar_mul(rstdg[:, :], rstdg2[:, :], -0.5)
                    v.wait_ge(sDv.h, sDv.n)

                @blk.scalar
                def _(_eng, ci=ci):
                    sc.wait_ge(sDv.h, sDv.n)
                    sc.activation(rstdgb[:, :], rstdg[:, :], AF.Exp)
                    sc.wait_ge(sAc.h, sAc.n)

                @blk.sync
                def _(s, ci=ci):
                    s.wait_ge(sAc.h, sAc.n)
                    s.dma_start(out=rstd_d, in_=rstdgb[:, :]).then_inc(sSt.inc(16), 16)
                    s.wait_ge(sSt.h, sSt.n)
                    s.dma_start(out=rrow[0:1, :],
                                in_=AP(rstd_d.tensor, 0, [[1024, 1], [1, 1024]]),
                                single_packet=True).then_inc(sSt.inc(16), 16)

                @blk.tensor
                def _(_eng, ci=ci):
                    t.wait_ge(sSt.h, sSt.n)
                    t.ldweights(onesb[0:1, :])
                    for g8 in range(8):
                        pb = pF[1] if g8 < 4 else pF[0]
                        off = (g8 % 4) * 128
                        t.matmul(pb[:, off:off + 128], onesb[0:1, :],
                                 rrow[0:1, g8 * 128:(g8 + 1) * 128],
                                 start=(g8 % 4 == 0), stop=(g8 % 4 == 3))
                    t.last.then_inc(sPe.inc(), 1)

                for wv in range(8):
                    @blk.vector
                    def _(_eng, ci=ci, wv=wv):
                        if wv == 0:
                            v.wait_ge(sPe.h, sPe.n)
                        else:
                            v.wait_ge(sSt.h, sSt.n)
                        for i in range(8):
                            j = wv * 8 + i
                            g8 = j // 8
                            pb = pF[1] if g8 < 4 else pF[0]
                            off = (g8 % 4) * 128
                            v.scalar_tensor_tensor(
                                nstag[:, i * 128:(i + 1) * 128],
                                gateb[:, j * 128:(j + 1) * 128],
                                gam_s[:, j:j + 1],
                                pb[:, off:off + 128],
                                ALU.mult, ALU.mult)
                        v.wait_ge(sDv.h, sDv.n)

                    @blk.sync
                    def _(s, ci=ci, wv=wv):
                        s.wait_ge(sDv.h, sDv.n)
                        dst = AP(normed_sp.tensor, wv * 1024 * SEG + (ci - CI0) * 128,
                                 [[SEG, 128], [128 * SEG, 8], [1, 128]])
                        src = AP(nstag, 0, [[1024, 128], [128, 8], [1, 128]])
                        s.dma_start(out=dst, in_=src).then_inc(sSt.inc(16), 16)

    if "C" not in MK_PASSES:
        _finish(nc, blk, [sAst[0], sAst[1], sSt, sOut])
        st.close()
        return nc
    stats1 = e(nc.sbuf_tensor("stats1", [1, 2 * SEG], F32))

    with ExitStack() as stC:
        eC = stC.enter_context
        nmh = eC(nc.sbuf_tensor("nmh", [128, 64 * 1024], BF16))
        w2b = [eC(nc.sbuf_tensor(f"w2b{i}", [128, 64 * 128], BF16)) for i in range(3)]
        peC = []
        dvZ = []
        stAdd = []
        xtl = [eC(nc.sbuf_tensor(f"xtl{i}", [128, 512], F32)) for i in range(2)]
        zst = [eC(nc.sbuf_tensor(f"zst{i}", [128, 512], F32)) for i in range(2)]
        zbb = eC(nc.sbuf_tensor("zbb", [128, 512], BF16))
        sqc = eC(nc.sbuf_tensor("sqc", [128, 512], BF16))
        pC = [eC(nc.psum_tensor(f"pC{i}", [128, 512], F32)) for i in range(2)]
        pS = [eC(nc.psum_tensor(f"pS{i}", [1, 512], F32)) for i in range(2)]

        @blk.vector
        def _(_eng):
            v.memset(stats1[:, :], 0.0)
            v.wait_ge(sDv.h, sDv.n)

        for th in range(2):
            @blk.sync
            def _(s, th=th):
                if th == 0:
                    s.wait_ge(sSt.h, sSt.n)
                if th == 1:
                    s.wait_ge(sPe.h, sPe.n)
                for i in range(8):
                    src = AP(normed_sp.tensor, i * 1024 * SEG + th * 1024,
                             [[SEG, 128], [128 * SEG, 8], [1, 1024]])
                    dst = AP(nmh, i * 8 * 1024, [[64 * 1024, 128], [1024, 8], [1, 1024]])
                    s.dma_start(out=dst, in_=src).then_inc(sNw.inc(16), 16)

            for oc in range(32):
                po = (th * 32 + oc) % 3

                @blk.gpsimd
                def _(_eng, th=th, oc=oc, po=po):
                    gi = th * 32 + oc
                    if gi >= 3:
                        g.wait_ge(sPe.h, peC[gi - 2])
                    src = AP(w2t.tensor, oc * INTER * 128,
                             [[128, 128], [128 * 128, 64], [1, 128]])
                    dst = AP(w2b[po], 0, [[64 * 128, 128], [128, 64], [1, 128]])
                    g.dma_start(out=dst, in_=src).then_inc(sCw[po].inc(16), 16)

                @blk.sync
                def _(s, th=th, oc=oc, po=po):
                    if th + oc > 0:
                        s.wait_ge(sDv.h, sDv.n)
                    for t2 in range(2):
                        col = 515 + th * 1024 + t2 * 512
                        s.dma_start(out=xtl[t2][:, :],
                                    in_=AP(xT.tensor, oc * 128 * SPAN + col,
                                           [[SPAN, 128], [1, 512]])
                                    ).then_inc(sCx[t2].inc(16), 16)

                @blk.tensor
                def _(_eng, th=th, oc=oc, po=po):
                    t.wait_ge(sCw[po].h, sCw[po].n)
                    if oc == 0:
                        t.wait_ge(sNw.h, sNw.n)
                    gi = th * 32 + oc
                    if gi >= 1:
                        t.wait_ge(sDv.h, dvZ[gi - 1])
                    for k in range(64):
                        t.ldweights(w2b[po][:, k * 128:(k + 1) * 128])
                        for t2 in range(2):
                            t.matmul(pC[t2][:, :],
                                     w2b[po][:, k * 128:(k + 1) * 128],
                                     nmh[:, k * 1024 + t2 * 512:k * 1024 + (t2 + 1) * 512],
                                     start=(k == 0), stop=(k == 63))
                    t.last.then_inc(sPe.inc(), 1)
                peC.append(sPe.n)

                for t2 in range(2):
                    @blk.vector
                    def _(_eng, th=th, oc=oc, t2=t2, po=po):
                        v.wait_ge(sPe.h, sPe.n)
                        v.wait_ge(sCx[t2].h, sCx[t2].n)
                        if th + oc >= 1:
                            v.wait_ge(sCst[t2].h, sCst[t2].n)
                        v.tensor_add(zst[t2][:, :], xtl[t2][:, :], pC[t2][:, :])
                        v.wait_ge(sDv.h, sDv.n)
                    if t2 == 1:
                        dvZ.append(sDv.n)

                    @blk.sync
                    def _(s, th=th, oc=oc, t2=t2):
                        s.wait_ge(sDv.h, sDv.n)
                        dst = AP(z_sp.tensor, oc * 128 * SEG + th * 1024 + t2 * 512,
                                 [[SEG, 128], [1, 512]])
                        s.dma_start(out=dst, in_=zst[t2][:, :]).then_inc(sCst[t2].inc(16), 16)

    if "D" not in MK_PASSES:
        _finish(nc, blk, [sAst[0], sAst[1], sCst[0], sCst[1], sSt, sOut])
        st.close()
        return nc
    with ExitStack() as stD:
        eD = stD.enter_context
        zb = eD(nc.sbuf_tensor("zb", [128, 32 * 512], F32))
        x1b = eD(nc.sbuf_tensor("x1b", [128, 32 * 512], BF16))
        ffnsb = eD(nc.sbuf_tensor("ffnsb", [128, 32 * 512], BF16))
        atb = eD(nc.sbuf_tensor("atb", [128, 32 * 128], BF16))
        bwtb = eD(nc.sbuf_tensor("bwtb", [128, 32 * 128], BF16))
        rwtb = eD(nc.sbuf_tensor("rwtb", [128, 32 * 8], BF16))
        lrow = eD(nc.sbuf_tensor("lrow", [1, 2 * 512], BF16))
        l8 = eD(nc.sbuf_tensor("l8", [8, 512], BF16))
        lT = eD(nc.sbuf_tensor("lT", [128, 8], BF16))
        mcol = eD(nc.sbuf_tensor("mcol", [128, 1], F32))
        rwq = eD(nc.sbuf_tensor("rwq", [128, 8], BF16))
        rwT = eD(nc.sbuf_tensor("rwT", [8, 512], BF16))
        rw16 = eD(nc.sbuf_tensor("rw16", [128, 512], BF16))
        whb = eD(nc.sbuf_tensor("whb", [128, 512], BF16))
        trow = eD(nc.sbuf_tensor("trow", [1, 512], F32))
        trow2 = eD(nc.sbuf_tensor("trow2", [1, 512], F32))
        stats2 = eD(nc.sbuf_tensor("stats2", [1, 2 * 512], F32))
        tq = [eD(nc.sbuf_tensor(f"tq{i}", [128, 512], F32)) for i in range(2)]
        ost = [eD(nc.sbuf_tensor(f"ost{i}", [128, 512], F32)) for i in range(2)]
        sq2 = eD(nc.sbuf_tensor("sq2", [128, 512], BF16))
        zb2 = eD(nc.sbuf_tensor("zb2", [128, 512], BF16))
        pD = [eD(nc.psum_tensor(f"pD{i}", [128, 512], F32)) for i in range(4)]
        pS2 = [eD(nc.psum_tensor(f"pS2{i}", [1, 512], F32)) for i in range(2)]
        pQ = eD(nc.psum_tensor("pQ", [128, 512], BF16))

        @blk.gpsimd
        def _(_eng):
            g.wait_ge(sPe.h, sPe.n)
            g.wait_ge(sDv.h, sDv.n)
            g.dma_start(out=AP(atb, 0, [[32 * 128, 128], [128, 32], [1, 128]]),
                        in_=AP(atw.tensor, 0, [[128, 128], [128 * 128, 32], [1, 128]])
                        ).then_inc(sDw.inc(16), 16)
            g.dma_start(out=bwtb[:, :], in_=bwt).then_inc(sDw.inc(16), 16)
            g.dma_start(out=AP(rwtb, 0, [[32 * 8, 128], [8, 32], [1, 8]]),
                        in_=AP(rwt.tensor, 0, [[8, 128], [128 * 8, 32], [1, 8]])
                        ).then_inc(sDw.inc(16), 16)

        for b4 in range(4):
            @blk.sync
            def _(s, b4=b4):
                if b4 == 0:
                    s.wait_ge(sCst[0].h, sCst[0].n)
                    s.wait_ge(sCst[1].h, sCst[1].n)
                if b4 >= 1:
                    s.wait_ge(sDv.h, sDv.n)
                for i in range(4):
                    src = AP(z_sp.tensor, i * 8 * 128 * SEG + b4 * 512,
                             [[SEG, 128], [128 * SEG, 8], [1, 512]])
                    dst = AP(zb, i * 8 * 512, [[32 * 512, 128], [512, 8], [1, 512]])
                    s.dma_start(out=dst, in_=src).then_inc(sZl.inc(16), 16)

            for oc in range(32):
                @blk.vector
                def _(_eng, b4=b4, oc=oc):
                    if oc == 0:
                        v.wait_ge(sZl.h, sZl.n)
                    else:
                        v.wait_ge(sPe.h, sPe.n)
                    v.tensor_copy(zb2[:, :], zb[:, oc * 512:(oc + 1) * 512])
                    v.wait_ge(sDv.h, sDv.n)

                @blk.scalar
                def _(_eng, b4=b4, oc=oc):
                    sc.wait_ge(sDv.h, sDv.n)
                    sc.activation(sq2[:, :], zb[:, oc * 512:(oc + 1) * 512], AF.Square)
                    sc.wait_ge(sAc.h, sAc.n)

                @blk.tensor
                def _(_eng, b4=b4, oc=oc):
                    t.wait_ge(sAc.h, sAc.n)
                    t.ldweights(onescol[:, 0:1])
                    t.matmul(pS2[0][:, :], onescol[:, 0:1], zb2[:, :],
                             start=(oc == 0), stop=(oc == 31))
                    t.matmul(pS2[1][:, :], onescol[:, 0:1], sq2[:, :],
                             start=(oc == 0), stop=(oc == 31))
                    t.last.then_inc(sPe.inc(), 1)

            @blk.vector
            def _(_eng, b4=b4):
                v.wait_ge(sPe.h, sPe.n)
                v.tensor_scalar_mul(trow[0:1, :], pS2[0][:, :], 1.0 / HID)
                v.tensor_scalar_mul(trow2[0:1, :], pS2[1][:, :], 1.0 / HID)
                v.wait_ge(sDv.h, sDv.n)
                v.tensor_tensor(stats2[0:1, 0:512], trow[0:1, :], trow[0:1, :],
                                ALU.mult)
                v.wait_ge(sDv.h, sDv.n)
                v.tensor_tensor(trow2[0:1, :], trow2[0:1, :], stats2[0:1, 0:512], ALU.subtract)
                v.wait_ge(sDv.h, sDv.n)

            @blk.scalar
            def _(_eng, b4=b4):
                sc.wait_ge(sDv.h, sDv.n)
                sc.activation(trow2[0:1, :], trow2[0:1, :], AF.Ln, bias=EPS)
                sc.wait_ge(sAc.h, sAc.n)

            @blk.vector
            def _(_eng, b4=b4):
                v.wait_ge(sAc.h, sAc.n)
                v.tensor_scalar_mul(trow2[0:1, :], trow2[0:1, :], -0.5)
                v.wait_ge(sDv.h, sDv.n)

            @blk.scalar
            def _(_eng, b4=b4):
                sc.wait_ge(sDv.h, sDv.n)
                sc.activation(trow2[0:1, :], trow2[0:1, :], AF.Exp)
                sc.wait_ge(sAc.h, sAc.n)

            @blk.vector
            def _(_eng, b4=b4):
                v.wait_ge(sAc.h, sAc.n)
                v.tensor_mul(trow[0:1, :], trow[0:1, :], trow2[0:1, :])
                v.tensor_copy(lrow[0:1, 0:512], trow2[0:1, :])
                v.wait_ge(sDv.h, sDv.n)
                v.tensor_tensor(stats2[0:1, 0:512], trow2[0:1, :], lrow[0:1, 0:512],
                                ALU.subtract)
                v.wait_ge(sDv.h, sDv.n)
                v.tensor_copy(lrow[0:1, 512:1024], stats2[0:1, 0:512])
                v.wait_ge(sDv.h, sDv.n)

            @blk.tensor
            def _(_eng, b4=b4):
                t.wait_ge(sDv.h, sDv.n)
                if b4 >= 1:
                    pass
                t.ldweights(onesb[0:1, :])
                t.matmul(pD[0][:, :], onesb[0:1, :], lrow[0:1, 0:512], start=True, stop=False)
                t.matmul(pD[0][:, :], onesb[0:1, :], lrow[0:1, 512:1024], start=False, stop=True)
                t.last.then_inc(sPe.inc(), 1)

            @blk.vector
            def _(_eng, b4=b4):
                v.wait_ge(sPe.h, sPe.n)
                v.tensor_copy(lrow[0:1, 0:512], trow[0:1, :])
                v.wait_ge(sDv.h, sDv.n)
                v.tensor_tensor(stats2[0:1, 0:512], trow[0:1, :], lrow[0:1, 0:512],
                                ALU.subtract)
                v.wait_ge(sDv.h, sDv.n)
                v.tensor_copy(lrow[0:1, 512:1024], stats2[0:1, 0:512])
                v.wait_ge(sDv.h, sDv.n)

            @blk.tensor
            def _(_eng, b4=b4):
                t.wait_ge(sDv.h, sDv.n)
                t.matmul(pD[1][:, :], onesb[0:1, :], lrow[0:1, 0:512], start=True, stop=False)
                t.matmul(pD[1][:, :], onesb[0:1, :], lrow[0:1, 512:1024], start=False, stop=True)
                t.last.then_inc(sPe.inc(), 1)

            if MK_DCUT < 2:
                continue
            for oc in range(32):
                @blk.vector
                def _(_eng, b4=b4, oc=oc):
                    if oc == 0:
                        v.wait_ge(sPe.h, sPe.n)
                    zt = zb[:, oc * 512:(oc + 1) * 512]
                    v.tensor_tensor(tq[oc % 2][:, :], zt, pD[0][:, :],
                                    ALU.mult)
                    v.wait_ge(sDv.h, sDv.n)
                    v.tensor_tensor(x1b[:, oc * 512:(oc + 1) * 512],
                                    tq[oc % 2][:, :], pD[1][:, :], ALU.subtract)
                    v.wait_ge(sDv.h, sDv.n)

            if MK_DCUT < 3:
                continue
            @blk.tensor
            def _(_eng, b4=b4):
                t.wait_ge(sDv.h, sDv.n)
                if b4 == 0:
                    t.wait_ge(sDw.h, sDw.n)
                for k in range(32):
                    t.ldweights(rwtb[:, k * 8:(k + 1) * 8])
                    t.matmul(pD[2][0:8, 0:512], rwtb[:, k * 8:(k + 1) * 8],
                             x1b[:, k * 512:(k + 1) * 512],
                             start=(k == 0), stop=(k == 31))
                t.last.then_inc(sPe.inc(), 1)

            @blk.vector
            def _(_eng, b4=b4):
                v.wait_ge(sPe.h, sPe.n)
                v.tensor_copy(l8[:, :], pD[2][0:8, 0:512])
                v.wait_ge(sDv.h, sDv.n)

            if MK_DCUT < 4:
                continue
            for q in range(4):
                @blk.tensor
                def _(_eng, b4=b4, q=q):
                    t.wait_ge(sDv.h, sDv.n)
                    t.transpose(pQ[0:128, 0:8], l8[:, q * 128:(q + 1) * 128], identb[0:8, 0:8])
                    t.last.then_inc(sPe.inc(), 1)

                @blk.vector
                def _(_eng, b4=b4, q=q):
                    v.wait_ge(sPe.h, sPe.n)
                    v.tensor_copy(lT[:, :], pQ[0:128, 0:8])
                    v.wait_ge(sDv.h, sDv.n)
                    v.reduce_max(mcol[:, :], lT[:, :], AX)
                    v.wait_ge(sDv.h, sDv.n)
                    v.tensor_scalar(rwq[:, :], lT[:, :], mcol[:, :], mcol[:, :],
                                    ALU.is_ge, ALU.mult)
                    v.wait_ge(sDv.h, sDv.n)

                @blk.tensor
                def _(_eng, b4=b4, q=q):
                    t.wait_ge(sDv.h, sDv.n)
                    t.transpose(pQ[0:8, 0:128], rwq[:, :], identb[:, :])
                    t.last.then_inc(sPe.inc(), 1)

                @blk.vector
                def _(_eng, b4=b4, q=q):
                    v.wait_ge(sPe.h, sPe.n)
                    v.tensor_copy(rwT[:, q * 128:(q + 1) * 128], pQ[0:8, 0:128])
                    v.wait_ge(sDv.h, sDv.n)

            if MK_DCUT < 5:
                continue
            @blk.tensor
            def _(_eng, b4=b4):
                t.wait_ge(sDv.h, sDv.n)
                t.ldweights(rep_s[0:8, :])
                t.matmul(pD[3][:, :], rep_s[0:8, :], rwT[:, :], start=True, stop=True)
                t.last.then_inc(sPe.inc(), 1)

            @blk.vector
            def _(_eng, b4=b4):
                v.wait_ge(sPe.h, sPe.n)
                v.tensor_copy(rw16[:, :], pD[3][:, :])
                v.wait_ge(sDv.h, sDv.n)

            @blk.tensor
            def _(_eng, b4=b4):
                t.wait_ge(sDv.h, sDv.n)
                for k in range(32):
                    t.ldweights(atb[:, k * 128:(k + 1) * 128])
                    t.matmul(pD[2][:, :], atb[:, k * 128:(k + 1) * 128],
                             x1b[:, k * 512:(k + 1) * 512],
                             start=(k == 0), stop=(k == 31))
                t.last.then_inc(sPe.inc(), 1)

            @blk.vector
            def _(_eng, b4=b4):
                v.wait_ge(sPe.h, sPe.n)
                v.tensor_tensor(whb[:, :], rw16[:, :], pD[2][:, :], ALU.mult)
                v.wait_ge(sDv.h, sDv.n)

            if MK_DCUT < 6:
                continue
            for oc in range(32):
                @blk.tensor
                def _(_eng, b4=b4, oc=oc):
                    if oc == 0:
                        t.wait_ge(sDv.h, sDv.n)
                    elif oc >= 2:
                        t.wait_ge(sDv.h, sDv.n)
                    t.ldweights(bwtb[:, oc * 128:(oc + 1) * 128])
                    t.matmul(pD[2 + oc % 2][:, :], bwtb[:, oc * 128:(oc + 1) * 128],
                             whb[:, :], start=True, stop=True)
                    t.last.then_inc(sPe.inc(), 1)

                @blk.vector
                def _(_eng, b4=b4, oc=oc):
                    v.wait_ge(sPe.h, sPe.n)
                    v.tensor_copy(ffnsb[:, oc * 512:(oc + 1) * 512], pD[2 + oc % 2][:, :])
                    v.wait_ge(sDv.h, sDv.n)

            if MK_DCUT < 7:
                continue
            @blk.vector
            def _(_eng, b4=b4):
                v.memset(stats2[:, :], 0.0)
                v.wait_ge(sDv.h, sDv.n)

            for oc in range(32):
                @blk.vector
                def _(_eng, b4=b4, oc=oc):
                    if oc >= 1:
                        v.wait_ge(sPe.h, sPe.n)
                    zt = zb[:, oc * 512:(oc + 1) * 512]
                    v.tensor_tensor(tq[0][:, :], zt, pD[0][:, :],
                                    ALU.mult)
                    v.wait_ge(sDv.h, sDv.n)
                    v.tensor_tensor(tq[1][:, :], tq[0][:, :], pD[1][:, :],
                                    ALU.subtract)
                    v.wait_ge(sDv.h, sDv.n)
                    v.tensor_add(zt, tq[1][:, :],
                                 ffnsb[:, oc * 512:(oc + 1) * 512])
                    v.wait_ge(sDv.h, sDv.n)
                    v.tensor_copy(zb2[:, :], zt)
                    v.wait_ge(sDv.h, sDv.n)

                @blk.scalar
                def _(_eng, b4=b4, oc=oc):
                    sc.wait_ge(sDv.h, sDv.n)
                    sc.activation(sq2[:, :], zb[:, oc * 512:(oc + 1) * 512], AF.Square)
                    sc.wait_ge(sAc.h, sAc.n)

                @blk.tensor
                def _(_eng, b4=b4, oc=oc):
                    t.wait_ge(sAc.h, sAc.n)
                    t.ldweights(onescol[:, 0:1])
                    t.matmul(pS2[0][:, :], onescol[:, 0:1], zb2[:, :],
                             start=(oc == 0), stop=(oc == 31))
                    t.matmul(pS2[1][:, :], onescol[:, 0:1], sq2[:, :],
                             start=(oc == 0), stop=(oc == 31))
                    t.last.then_inc(sPe.inc(), 1)

            if MK_DCUT < 8:
                continue
            @blk.vector
            def _(_eng, b4=b4):
                v.wait_ge(sPe.h, sPe.n)
                v.tensor_scalar_mul(trow[0:1, :], pS2[0][:, :], 1.0 / HID)
                v.tensor_scalar_mul(trow2[0:1, :], pS2[1][:, :], 1.0 / HID)
                v.wait_ge(sDv.h, sDv.n)
                v.tensor_tensor(stats2[0:1, 0:512], trow[0:1, :], trow[0:1, :], ALU.mult)
                v.wait_ge(sDv.h, sDv.n)
                v.tensor_tensor(trow2[0:1, :], trow2[0:1, :], stats2[0:1, 0:512], ALU.subtract)
                v.wait_ge(sDv.h, sDv.n)

            @blk.scalar
            def _(_eng, b4=b4):
                sc.wait_ge(sDv.h, sDv.n)
                sc.activation(trow2[0:1, :], trow2[0:1, :], AF.Ln, bias=EPS)
                sc.wait_ge(sAc.h, sAc.n)

            @blk.vector
            def _(_eng, b4=b4):
                v.wait_ge(sAc.h, sAc.n)
                v.tensor_scalar_mul(trow2[0:1, :], trow2[0:1, :], -0.5)
                v.wait_ge(sDv.h, sDv.n)

            @blk.scalar
            def _(_eng, b4=b4):
                sc.wait_ge(sDv.h, sDv.n)
                sc.activation(trow2[0:1, :], trow2[0:1, :], AF.Exp)
                sc.wait_ge(sAc.h, sAc.n)

            @blk.vector
            def _(_eng, b4=b4):
                v.wait_ge(sAc.h, sAc.n)
                v.tensor_mul(trow[0:1, :], trow[0:1, :], trow2[0:1, :])
                v.tensor_copy(lrow[0:1, 0:512], trow2[0:1, :])
                v.wait_ge(sDv.h, sDv.n)
                v.tensor_tensor(stats2[0:1, 0:512], trow2[0:1, :], lrow[0:1, 0:512], ALU.subtract)
                v.wait_ge(sDv.h, sDv.n)
                v.tensor_copy(lrow[0:1, 512:1024], stats2[0:1, 0:512])
                v.wait_ge(sDv.h, sDv.n)

            @blk.tensor
            def _(_eng, b4=b4):
                t.wait_ge(sDv.h, sDv.n)
                t.ldweights(onesb[0:1, :])
                t.matmul(pD[0][:, :], onesb[0:1, :], lrow[0:1, 0:512], start=True, stop=False)
                t.matmul(pD[0][:, :], onesb[0:1, :], lrow[0:1, 512:1024], start=False, stop=True)
                t.last.then_inc(sPe.inc(), 1)

            @blk.vector
            def _(_eng, b4=b4):
                v.wait_ge(sPe.h, sPe.n)
                v.tensor_copy(lrow[0:1, 0:512], trow[0:1, :])
                v.wait_ge(sDv.h, sDv.n)
                v.tensor_tensor(stats2[0:1, 0:512], trow[0:1, :], lrow[0:1, 0:512], ALU.subtract)
                v.wait_ge(sDv.h, sDv.n)
                v.tensor_copy(lrow[0:1, 512:1024], stats2[0:1, 0:512])
                v.wait_ge(sDv.h, sDv.n)

            @blk.tensor
            def _(_eng, b4=b4):
                t.wait_ge(sDv.h, sDv.n)
                t.matmul(pD[1][:, :], onesb[0:1, :], lrow[0:1, 0:512], start=True, stop=False)
                t.matmul(pD[1][:, :], onesb[0:1, :], lrow[0:1, 512:1024], start=False, stop=True)
                t.last.then_inc(sPe.inc(), 1)

            for oc in range(32):
                @blk.vector
                def _(_eng, b4=b4, oc=oc):
                    if oc == 0:
                        v.wait_ge(sPe.h, sPe.n)
                    if b4 > 0 or oc >= 2:
                        v.wait_ge(sOut.h, sOut.n)
                    zt = zb[:, oc * 512:(oc + 1) * 512]
                    v.tensor_tensor(tq[oc % 2][:, :], zt, pD[0][:, :],
                                    ALU.mult)
                    v.wait_ge(sDv.h, sDv.n)
                    v.tensor_tensor(ost[oc % 2][:, :], tq[oc % 2][:, :], pD[1][:, :],
                                    ALU.subtract)
                    v.wait_ge(sDv.h, sDv.n)

                @blk.sync
                def _(s, b4=b4, oc=oc):
                    s.wait_ge(sDv.h, sDv.n)
                    dst = AP(outT.tensor, oc * 128 * SEG + b4 * 512,
                             [[SEG, 128], [1, 512]])
                    s.dma_start(out=dst, in_=ost[oc % 2][:, :]).then_inc(sOut.inc(16), 16)

        @blk.sync
        def _(s):
            s.wait_ge(sOut.h, sOut.n)
            s.wait_ge(sAst[0].h, sAst[0].n)
            s.wait_ge(sAst[1].h, sAst[1].n)
            s.wait_ge(sCst[0].h, sCst[0].n)
            s.wait_ge(sCst[1].h, sCst[1].n)
            s.wait_ge(sSt.h, sSt.n)

    st.close()
    return nc


def _prep_inputs(inputs):
    x = np.asarray(inputs["L0_attn_hidden_states"])
    w1 = np.asarray(inputs["L0_attn_in_proj_w"])
    w2 = np.asarray(inputs["L0_attn_out_proj_w"])
    cw = np.asarray(inputs["L0_attn_conv_w"])
    cb = np.asarray(inputs["L0_attn_conv_b"])
    dtb = np.asarray(inputs["L0_attn_dt_bias"])
    al = np.asarray(inputs["L0_attn_A_log"])
    dd = np.asarray(inputs["L0_attn_D"])
    nw = np.asarray(inputs["L0_attn_norm_weight"])
    rw = np.asarray(inputs["L0_ffn_route_w"])
    aw = np.asarray(inputs["L0_ffn_A"])
    bw = np.asarray(inputs["L0_ffn_Bw"])

    w1t = np.ascontiguousarray(w1.reshape(JC_ALL, 128, HID).transpose(0, 2, 1))
    w2t = np.ascontiguousarray(w2.reshape(32, 128, INTER).transpose(0, 2, 1))
    convw = np.ascontiguousarray(
        cw.reshape(80, 128, 4).transpose(1, 0, 2).reshape(128, 320))
    convb = np.ascontiguousarray(cb.reshape(80, 128).T)
    dexp = np.empty((128, 64), np.float32)
    dexp[:64, :] = dd[0::2][None, :]
    dexp[64:, :] = dd[1::2][None, :]
    gam = np.ascontiguousarray(nw.reshape(64, 128).T)
    rwt = np.ascontiguousarray(rw.T)
    atw = np.ascontiguousarray(aw.reshape(128, HID).T)
    bwt = np.ascontiguousarray(bw.transpose(0, 2, 1).reshape(128, HID))

    in_maps = []
    for c in range(NC):
        b, s = divmod(c, 4)
        t0 = s * SEG - (HALO + 3)
        if t0 < 0:
            xs = np.concatenate([x[b, 0:HALO + 3], x[b, 0:SEG]], axis=0)
        else:
            xs = x[b, t0:t0 + SPAN]
        xTc = np.ascontiguousarray(xs.T)
        hm = np.full(128, 0.0 if s == 0 else 1.0, np.float32)
        in_maps.append(dict(
            xT=xTc, w1t=w1t, w2t=w2t, convw=convw, convb=convb,
            dtbias=dtb, alog=al, dexp=dexp, gam=gam, rwt=rwt, atw=atw,
            bwt=bwt, hmask=hm))
    return in_maps


def kernel(**inputs):
    if "nc" not in _CACHE:
        _CACHE["nc"] = build_nc()
    nc = _CACHE["nc"]
    in_maps = _prep_inputs(inputs)
    res = run_bass_kernel_spmd(nc, in_maps, core_ids=list(range(NC)))
    out = np.empty((2, 2 * SEG * 4 // 2, HID), np.float32)
    out = np.empty((2, 8192, HID), np.float32)
    for c in range(NC):
        b, s = divmod(c, 4)
        out[b, s * SEG:(s + 1) * SEG, :] = res.results[c]["outT"].T
    if DEBUG_TAPS:
        kernel.taps = res.results
    return out
